# version 02
# baseline (speedup 1.0000x reference)
\
import numpy as np
import ml_dtypes
import concourse.bass as bass
import concourse.mybir as mybir
from concourse.bass_utils import run_bass_kernel_spmd
IndirectOffsetOnAxis = bass.IndirectOffsetOnAxis

F32 = mybir.dt.float32
BF16 = mybir.dt.bfloat16
I32 = mybir.dt.int32
AF = mybir.ActivationFunctionType
ALU = mybir.AluOpType
AX = mybir.AxisListType

D = 1024
T = 2048
BC = 2
NT = BC * T
NC = D // 128
NCORE = 8

class Ctx:
    ENGINES = ("pe", "dve", "act", "pool", "sp")

    def __init__(self, nc):
        self.nc = nc
        self.q = {e: [] for e in self.ENGINES}
        self.cnt = {}
        self.sems = {}
        self.sem_names = ["pe", "dve", "act", "pool", "dma_p", "dma_s"]

    def _inc(self, sem, by):
        self.cnt[sem] = self.cnt.get(sem, 0) + by
        return self.cnt[sem]

    def emit(self, eng, fn):
        self.q[eng].append(fn)

    def wait(self, eng, sem, val):
        if val is None or val <= 0:
            return
        self.emit(eng, lambda e, s=sem, v=val: e.wait_ge(self.sems[s], v))

    def op(self, eng, fn, inc=True):
        if inc:
            if eng in ("dve", "act", "pool"):
                self.wait(eng, eng, self.c(eng))
            self._inc(eng, 1)
            def run(e, fn=fn, sem=eng):
                fn(e).then_inc(self.sems[sem], 1)
            self.emit(eng, run)
        else:
            self.emit(eng, lambda e, fn=fn: fn(e))
        return self.cnt.get(eng, 0)

    def ops_last_inc(self, eng, fns):
        for f in fns[:-1]:
            self.emit(eng, lambda e, f=f: f(e))
        return self.op(eng, fns[-1])

    def dma(self, queue, out_ap, in_ap, **kw):
        sem = "dma_p" if queue == "pool" else "dma_s"
        v = self._inc(sem, 16)
        def run(e, o=out_ap, i=in_ap, kw=kw, sem=sem):
            e.dma_start(o, i, **kw).then_inc(self.sems[sem], 16)
        self.emit("pool" if queue == "pool" else "sp", run)
        return v

    def idma(self, out_ap, out_off, in_ap, in_off, **kw):
        sem = "dma_p"
        v = self._inc(sem, 16)
        def run(e, kw=kw):
            e.indirect_dma_start(out_ap, out_off, in_ap, in_off, **kw).then_inc(
                self.sems[sem], 16)
        self.emit("pool", run)
        return v

    def c(self, sem):
        return self.cnt.get(sem, 0)

    def barrier(self, eng):
        for s in ("pe", "dve", "act", "pool", "dma_p", "dma_s"):
            if s != eng:
                self.wait(eng, s, self.c(s))

    def barrier_all(self):
        for e in self.ENGINES:
            self.barrier(e)




import contextlib

def salloc(stack, nc, name, shape, dt):
    return stack.enter_context(nc.sbuf_tensor(name, shape, dt)).ap()


def palloc(stack, nc, name, shape, dt=F32):
    return stack.enter_context(nc.psum_tensor(name, shape, dt)).ap()

ENG_ATTR = {"pe": "tensor", "dve": "vector", "act": "scalar",
            "pool": "gpsimd", "sp": "sync"}


def finalize(ctx):
    nc = ctx.nc
    import contextlib
    stack = contextlib.ExitStack()
    for name in ctx.sem_names:
        ctx.sems[name] = stack.enter_context(nc.semaphore(name))
    with stack:
        with nc.Block() as block:
            for eng in ctx.ENGINES:
                fns = ctx.q[eng]
                if not fns:
                    continue
                dec = getattr(block, ENG_ATTR[eng])
                def make(fns):
                    def body(e):
                        for f in fns:
                            f(e)
                    return body
                dec(make(fns))


def host_constants():
    c = {}
    c["c_ident"] = np.eye(128, dtype=ml_dtypes.bfloat16)
    c["c_identf"] = np.eye(128, dtype=np.float32)
    c["c_ones"] = np.ones((128, 128), dtype=ml_dtypes.bfloat16)
    u = np.triu(np.ones((64, 64), dtype=np.float32))
    c["c_cumsum_neg"] = (-u).astype(np.float32)
    c["c_mask_f"] = np.triu(np.ones((64, 64), dtype=np.float32))
    c["c_mask_b"] = np.tril(np.ones((64, 64), dtype=np.float32), -1)
    c["c_prefix"] = np.triu(np.ones((128, 128)), 1).astype(ml_dtypes.bfloat16)
    c["c_prefix32"] = np.triu(np.ones((128, 128), dtype=np.float32), 1)
    c["c_ones32"] = np.ones((128, 128), dtype=np.float32)
    c["c_iota128"] = np.arange(128, dtype=np.float32).reshape(128, 1)
    c["c_iota8"] = np.broadcast_to(np.arange(8, dtype=np.float32), (128, 8)).copy()
    c["c_fscal"] = np.broadcast_to(np.array(FSCAL_VALS, dtype=np.float32), (128, len(FSCAL_VALS))).copy()
    b = np.zeros((128, 128), dtype=ml_dtypes.bfloat16)
    b[0:64, 0:64] = 1; b[64:128, 64:128] = 1
    c["c_blkones"] = b
    sz = np.zeros((16, 16 * 64), dtype=ml_dtypes.bfloat16)
    for h in range(16):
        sz[h, h * 64:(h + 1) * 64] = 1
    c["c_selz64"] = sz
    sb = np.zeros((96, 16 * 128), dtype=ml_dtypes.bfloat16)
    sc_ = np.zeros((96, 16 * 128), dtype=ml_dtypes.bfloat16)
    for n in range(16):
        sb[64 + n, n * 128:(n + 1) * 128] = 1
        sc_[80 + n, n * 128:(n + 1) * 128] = 1
    c["c_seln"] = sb
    c["c_selc"] = sc_
    c["c_initidx"] = np.full((8 * 1024,), 1 << 30, dtype=np.int32)
    return c


FSCAL_VALS = [1e-5, 6.4e-4, 1e-12, 1e-6, 1.0, 0.0, -60.0, 1.2]
FS = {v: i for i, v in enumerate(FSCAL_VALS)}

CONST_SPECS = {
    "ident": ("c_ident", [128, 128], BF16),
    "identf": ("c_identf", [128, 128], F32),
    "ones": ("c_ones", [128, 128], BF16),
    "cumsum_neg": ("c_cumsum_neg", [64, 64], F32),
    "mask_f": ("c_mask_f", [64, 64], F32),
    "mask_b": ("c_mask_b", [64, 64], F32),
    "prefix": ("c_prefix", [128, 128], BF16),
    "prefix32": ("c_prefix32", [128, 128], F32),
    "ones32": ("c_ones32", [128, 128], F32),
    "iota128": ("c_iota128", [128, 1], F32),
    "iota8": ("c_iota8", [128, 8], F32),
    "fscal": ("c_fscal", [128, len(FSCAL_VALS)], F32),
    "blkones": ("c_blkones", [128, 128], BF16),
    "selz64": ("c_selz64", [16, 16 * 64], BF16),
    "seln": ("c_seln", [96, 16 * 128], BF16),
    "selc": ("c_selc", [96, 16 * 128], BF16),
}


class Consts:
    def __init__(self, ctx):
        nc = ctx.nc
        for name, (iname, shape, dt) in CONST_SPECS.items():
            ap = nc.dram_tensor(iname, shape, dt, kind="ExternalInput").ap()
            sb = nc.alloc_sbuf_tensor("k_" + name, shape, dt).ap()
            ctx.dma("sp", sb, ap)
            setattr(self, name, sb)
        self.ready = ctx.c("dma_s")


def emit_ln(ctx, cst, src_fm, dst_fm, gamma=None, beta=None, dst_tm=None,
            ntok=NT, eps=1e-5, src_dt=F32, res_in=None, res_out=None,
            dst_f32=None, name="ln"):
    nc = ctx.nc
    TT = 512
    nt = ntok // TT
    ctx.barrier_all()
    stack = contextlib.ExitStack()
    xs = salloc(stack, nc, f"{name}_xs", [128, NC * TT], src_dt)
    xb = salloc(stack, nc, f"{name}_xb", [128, NC * TT], BF16)
    xq = salloc(stack, nc, f"{name}_xq", [128, NC * TT], BF16)
    mean = salloc(stack, nc, f"{name}_mean", [128, TT], F32)
    msq = salloc(stack, nc, f"{name}_msq", [128, TT], F32)
    var = salloc(stack, nc, f"{name}_var", [128, TT], F32)
    rstd = salloc(stack, nc, f"{name}_rstd", [128, TT], F32)
    xo = salloc(stack, nc, f"{name}_xo", [128, NC * TT],
                F32 if res_out is not None else BF16)
    rt = salloc(stack, nc, f"{name}_rt", [128, NC * TT], F32) if res_out is not None else None
    stg = salloc(stack, nc, f"{name}_stg", [128, 2 * 512], BF16)
    ps = palloc(stack, nc, f"{name}_ps", [128, 2 * 512], F32)
    pt = palloc(stack, nc, f"{name}_pt", [128, 512], BF16) if dst_tm is not None else None

    src_r = src_fm.rearrange("(c p) t -> p c t", p=128)
    dst_r = dst_fm.rearrange("(c p) t -> p c t", p=128) if dst_fm is not None else None
    dst32_r = dst_f32.rearrange("(c p) t -> p c t", p=128) if dst_f32 is not None else None
    resi_r = res_in.rearrange("(c p) t -> p c t", p=128) if res_in is not None else None
    reso_r = res_out.rearrange("(c p) t -> p c t", p=128) if res_out is not None else None

    stg_par = 0
    stg_dma = [0, 0]
    for it in range(nt):
        t0 = it * TT
        ctx.wait("sp", "dve", ctx.c("dve"))
        ctx.wait("sp", "act", ctx.c("act"))
        dv = ctx.dma("sp", xs.rearrange("p (c t) -> p c t", c=NC),
                     src_r[:, :, t0:t0 + TT])
        if res_in is not None:
            dv = ctx.dma("sp", rt.rearrange("p (c t) -> p c t", c=NC),
                         resi_r[:, :, t0:t0 + TT])
        ctx.wait("dve", "dma_s", dv)
        ctx.wait("act", "dma_s", dv)
        c_xb = ctx.op("dve", lambda v: v.tensor_copy(xb[:, :], xs[:, :]))
        c_xq = ctx.op("act", lambda s: s.activation(xq[:, :], xs[:, :], AF.Square))
        ctx.wait("pe", "dve", c_xb)
        ctx.wait("pe", "act", c_xq)
        fns = []
        for k in range(NC):
            fns.append(lambda t, k=k: t.matmul(ps[:, 0:512], cst.ones[:, :],
                                               xb[:, k * TT:(k + 1) * TT],
                                               start=(k == 0), stop=(k == NC - 1)))
        for k in range(NC):
            fns.append(lambda t, k=k: t.matmul(ps[:, 512:1024], cst.ones[:, :],
                                               xq[:, k * TT:(k + 1) * TT],
                                               start=(k == 0), stop=(k == NC - 1)))
        c_mm = ctx.ops_last_inc("pe", fns)
        ctx.wait("act", "pe", c_mm)
        ctx.op("act", lambda s: s.activation(mean[:, :], ps[:, 0:512], AF.Copy, scale=1.0 / D))
        c_msq = ctx.op("act", lambda s: s.activation(msq[:, :], mean[:, :], AF.Square))
        ctx.wait("dve", "act", c_msq)
        c_var = ctx.op("dve", lambda v: v.scalar_tensor_tensor(
            var[:, :], ps[:, 512:1024], 1.0 / D, msq[:, :], ALU.mult, ALU.subtract))
        ctx.wait("act", "dve", c_var)
        ctx.op("act", lambda s: s.activation(rstd[:, :], var[:, :], AF.Ln,
                                             bias=cst.fscal[:, FS[eps]:FS[eps] + 1]))
        c_rstd = ctx.op("act", lambda s: s.activation(rstd[:, :], rstd[:, :], AF.Exp, scale=-0.5))
        ctx.wait("dve", "act", c_rstd)
        tgt = xs if src_dt == F32 else xo
        for k in range(NC):
            sl = slice(k * TT, (k + 1) * TT)
            if src_dt == F32:
                ctx.op("dve", lambda v, sl=sl: v.tensor_sub(
                    xs[:, sl], xs[:, sl], mean[:, :]), inc=False)
                sub_src = xs
            else:
                ctx.op("dve", lambda v, sl=sl: v.tensor_sub(
                    xo[:, sl], xs[:, sl], mean[:, :]), inc=False)
                sub_src = xo
            if gamma is None:
                if dst_f32 is not None:
                    ctx.op("dve", lambda v, sl=sl, sub_src=sub_src: v.tensor_mul(
                        xs[:, sl], sub_src[:, sl], rstd[:, :]), inc=False)
                    ctx.op("dve", lambda v, sl=sl: v.tensor_copy(
                        xo[:, sl], xs[:, sl]))
                else:
                    ctx.op("dve", lambda v, sl=sl, sub_src=sub_src: v.tensor_mul(
                        xo[:, sl], sub_src[:, sl], rstd[:, :]))
            else:
                ctx.op("dve", lambda v, sl=sl, sub_src=sub_src: v.tensor_mul(
                    xo[:, sl], sub_src[:, sl], rstd[:, :]), inc=False)
                ctx.op("dve", lambda v, k=k, sl=sl: v.tensor_scalar(
                    xo[:, sl], xo[:, sl],
                    gamma[:, k:k + 1], beta[:, k:k + 1], ALU.mult, ALU.add))
            if res_out is not None:
                ctx.op("dve", lambda v, sl=sl: v.tensor_add(
                    rt[:, sl], rt[:, sl], xo[:, sl]))
        c_xo = ctx.c("dve")
        ctx.wait("sp", "dve", c_xo)
        if res_out is not None:
            ctx.dma("sp", reso_r[:, :, t0:t0 + TT], rt.rearrange("p (c t) -> p c t", c=NC))
        else:
            ctx.dma("sp", dst_r[:, :, t0:t0 + TT], xo.rearrange("p (c t) -> p c t", c=NC))
            if dst_f32 is not None:
                ctx.dma("sp", dst32_r[:, :, t0:t0 + TT],
                        xs.rearrange("p (c t) -> p c t", c=NC))
        if dst_tm is not None:
            ctx.wait("pe", "dve", c_xo)
            for k in range(NC):
                fns = []
                for j in range(4):
                    fns.append(lambda t, k=k, j=j: t.transpose(
                        pt[:, j * 128:(j + 1) * 128],
                        xo[:, k * TT + j * 128:k * TT + (j + 1) * 128],
                        cst.ident[:, :]))
                c_tr = ctx.ops_last_inc("pe", fns)
                ctx.wait("dve", "pe", c_tr)
                ctx.wait("dve", "dma_s", stg_dma[stg_par])
                c_ev = ctx.op("dve", lambda v, sp=stg_par: v.tensor_copy(
                    stg[:, sp * 512:(sp + 1) * 512], pt[:, :]))
                ctx.wait("sp", "dve", c_ev)
                stg_dma[stg_par] = ctx.dma(
                    "sp",
                    dst_tm[t0:t0 + 512, k * 128:(k + 1) * 128].rearrange(
                        "(j p) c -> p j c", p=128),
                    stg[:, stg_par * 512:(stg_par + 1) * 512].rearrange(
                        "p (j c) -> p j c", j=4))
                ctx.wait("pe", "dve", c_ev)
                stg_par ^= 1
    ctx.barrier_all()
    stack.close()


def emit_ffn(ctx, cst, xn_fm, res_in, res_out, w1T, w2T, h1T, h2T, w3T,
             s1, s2, hs1, hs2, s3, h_d, g_d, name="ffn"):
    nc = ctx.nc
    DH = 4096
    NH = DH // 128
    HALF = NT // 2

    ctx.barrier_all()
    stack = contextlib.ExitStack()
    wpan = [salloc(stack, nc, f"{name}_wp{i}", [128, 16 * 128], BF16)
            for i in range(2)]
    ppan = [salloc(stack, nc, f"{name}_pp{i}", [128, NH * 128], BF16)
            for i in range(2)]
    atmp = salloc(stack, nc, f"{name}_atmp", [128, 4 * 512], F32)
    htile = [salloc(stack, nc, f"{name}_ht{i}", [128, 4 * 512], BF16)
             for i in range(2)]
    rtile = [salloc(stack, nc, f"{name}_rt{i}", [128, 4 * 512], F32)
             for i in range(2)]
    psA = palloc(stack, nc, f"{name}_psA", [128, 4 * 512], F32)
    psB = palloc(stack, nc, f"{name}_psB", [128, 4 * 512], F32)
    stack1 = contextlib.ExitStack()
    xn_half = salloc(stack1, nc, f"{name}_xnh", [128, NC * HALF], BF16)

    xn_r = xn_fm.rearrange("(c p) t -> p c t", p=128)
    h_r = h_d.rearrange("(c p) t -> p c t", p=128)
    g_r = g_d.rearrange("(c p) t -> p c t", p=128)
    res_in_r = res_in.rearrange("(c p) t -> p c t", p=128)
    res_out_r = res_out.rearrange("(c p) t -> p c t", p=128)

    pan_use = {0: 0, 1: 0}
    stg_use = {0: 0, 1: 0}

    for half in range(2):
        t0 = half * HALF
        ctx.wait("sp", "pe", ctx.c("pe"))
        dv = ctx.dma("sp", xn_half.rearrange("p (c t) -> p c t", c=NC),
                     xn_r[:, :, t0:t0 + HALF])
        ctx.wait("pe", "dma_s", dv)
        for o in range(NH):
            par = o % 2
            pb = wpan[par]
            ctx.wait("pool", "pe", pan_use[par])
            ctx.dma("pool", pb[:, 0:8 * 128].rearrange("p (k c) -> p k c", k=8),
                    w1T[:, o * 128:(o + 1) * 128].rearrange("(k p) c -> p k c", p=128))
            dv2 = ctx.dma("pool", pb[:, 8 * 128:16 * 128].rearrange("p (k c) -> p k c", k=8),
                          w2T[:, o * 128:(o + 1) * 128].rearrange("(k p) c -> p k c", p=128))
            ctx.wait("pe", "dma_p", dv2)
            fns = []
            for k in range(8):
                for t4 in range(4):
                    fns.append(lambda t, k=k, t4=t4: t.matmul(
                        psA[:, t4 * 512:(t4 + 1) * 512],
                        pb[:, k * 128:(k + 1) * 128],
                        xn_half[:, k * HALF + t4 * 512:k * HALF + (t4 + 1) * 512],
                        start=(k == 0), stop=(k == 7)))
                for t4 in range(4):
                    fns.append(lambda t, k=k, t4=t4: t.matmul(
                        psB[:, t4 * 512:(t4 + 1) * 512],
                        pb[:, (8 + k) * 128:(9 + k) * 128],
                        xn_half[:, k * HALF + t4 * 512:k * HALF + (t4 + 1) * 512],
                        start=(k == 0), stop=(k == 7)))
            c_mm = ctx.ops_last_inc("pe", fns)
            pan_use[par] = c_mm
            ctx.wait("act", "pe", c_mm)
            ctx.wait("act", "dve", ctx.c("dve"))
            c_a = ctx.op("act", lambda s, o=o: s.activation(
                atmp[:, :], psA[:, :], AF.Silu, scale=s1[:, o:o + 1]))
            hpar = o % 2
            ctx.wait("dve", "act", c_a)
            ctx.wait("dve", "dma_s", stg_use[hpar])
            c_h = ctx.op("dve", lambda v, o=o, hpar=hpar: v.scalar_tensor_tensor(
                htile[hpar][:, :], psB[:, :], s2[:, o:o + 1], atmp[:, :],
                ALU.mult, ALU.mult))
            ctx.wait("sp", "dve", c_h)
            stg_use[hpar] = ctx.dma("sp", h_r[:, o, t0:t0 + HALF],
                                    htile[hpar][:, 0:HALF])
            ctx.wait("pe", "act", c_a)
            ctx.wait("pe", "dve", c_h)

    ctx.barrier_all()
    stack1.close()
    hbuf = salloc(stack, nc, f"{name}_hbuf", [128, NH * HALF], BF16)
    evac_done = {0: 0, 1: 0}
    for stage, (wT, hsX, src_r, dst_r2) in enumerate(
            [(h1T, hs1, h_r, g_r), (h2T, hs2, g_r, h_r)]):
        for half in range(2):
            t0 = half * HALF
            ctx.wait("sp", "pe", ctx.c("pe"))
            dv = ctx.dma("sp", hbuf.rearrange("p (c t) -> p c t", c=NH),
                         src_r[:, :, t0:t0 + HALF])
            ctx.wait("pe", "dma_s", dv)
            for o in range(NH):
                par = o % 2
                pb = ppan[par]
                psX = psA if par == 0 else psB
                ctx.wait("pool", "pe", pan_use[par])
                dvw = ctx.dma("pool", pb.rearrange("p (k c) -> p k c", k=NH),
                              wT[:, o * 128:(o + 1) * 128].rearrange("(k p) c -> p k c", p=128))
                ctx.wait("pe", "dma_p", dvw)
                ctx.wait("pe", "act", evac_done[par])
                fns = []
                for k in range(NH):
                    for t4 in range(4):
                        fns.append(lambda t, k=k, t4=t4, psX=psX: t.matmul(
                            psX[:, t4 * 512:(t4 + 1) * 512],
                            pb[:, k * 128:(k + 1) * 128],
                            hbuf[:, k * HALF + t4 * 512:k * HALF + (t4 + 1) * 512],
                            start=(k == 0), stop=(k == NH - 1)))
                c_mm = ctx.ops_last_inc("pe", fns)
                pan_use[par] = c_mm
                ctx.wait("act", "pe", c_mm)
                ctx.wait("act", "dma_s", stg_use[par])
                c_g = ctx.op("act", lambda s, o=o, hsX=hsX, psX=psX, par=par: s.activation(
                    htile[par][:, :], psX[:, :], AF.Gelu, scale=hsX[:, o:o + 1]))
                evac_done[par] = c_g
                ctx.wait("sp", "act", c_g)
                stg_use[par] = ctx.dma("sp", dst_r2[:, o, t0:t0 + HALF],
                                       htile[par][:, 0:HALF])

    rt_use = {0: 0, 1: 0}
    for half in range(2):
        t0 = half * HALF
        ctx.wait("sp", "pe", ctx.c("pe"))
        dv = ctx.dma("sp", hbuf.rearrange("p (c t) -> p c t", c=NH),
                     h_r[:, :, t0:t0 + HALF])
        ctx.wait("pe", "dma_s", dv)
        for o in range(NC):
            par = o % 2
            pb = ppan[par]
            psX = psA if par == 0 else psB
            ctx.wait("pool", "pe", pan_use[par])
            dvw = ctx.dma("pool", pb.rearrange("p (k c) -> p k c", k=NH),
                          w3T[:, o * 128:(o + 1) * 128].rearrange("(k p) c -> p k c", p=128))
            ctx.wait("sp", "dma_s", rt_use[par])
            ctx.wait("sp", "dve", ctx.c("dve"))
            dvr = ctx.dma("sp", rtile[par][:, :], res_in_r[:, o, t0:t0 + HALF])
            ctx.wait("pe", "dma_p", dvw)
            ctx.wait("pe", "act", evac_done[par])
            fns = []
            for k in range(NH):
                for t4 in range(4):
                    fns.append(lambda t, k=k, t4=t4, psX=psX: t.matmul(
                        psX[:, t4 * 512:(t4 + 1) * 512],
                        pb[:, k * 128:(k + 1) * 128],
                        hbuf[:, k * HALF + t4 * 512:k * HALF + (t4 + 1) * 512],
                        start=(k == 0), stop=(k == NH - 1)))
            c_mm = ctx.ops_last_inc("pe", fns)
            pan_use[par] = c_mm
            ctx.wait("act", "pe", c_mm)
            ctx.wait("act", "dve", ctx.c("dve"))
            c_y = ctx.op("act", lambda s, o=o, psX=psX: s.activation(
                atmp[:, :], psX[:, :], AF.Copy, scale=s3[:, o:o + 1]))
            evac_done[par] = c_y
            ctx.wait("dve", "act", c_y)
            ctx.wait("dve", "dma_s", dvr)
            c_o = ctx.op("dve", lambda v, par=par: v.tensor_add(
                rtile[par][:, :], rtile[par][:, :], atmp[:, :]))
            ctx.wait("sp", "dve", c_o)
            rt_use[par] = ctx.dma("sp", res_out_r[:, o, t0:t0 + HALF], rtile[par][:, :])
    ctx.barrier_all()
    stack.close()


def _unused_prep(vec):
    n = vec.shape[0] // 128
    return np.ascontiguousarray(vec.reshape(n, 128).T.astype(np.float32))


def load_scale_col(ctx, name, n):
    nc = ctx.nc
    ap = nc.dram_tensor(name, [128, n], F32, kind="ExternalInput").ap()
    sb = nc.alloc_sbuf_tensor("sb_" + name, [128, n], F32).ap()
    ctx.dma("sp", sb, ap)
    return sb


def kernel(**inputs):
    raise NotImplementedError("under construction")


def emit_mla(ctx, cst, xn_fm, res_in, res_out, W, name="mla", dbg=None):
    nc = ctx.nc
    NH = 8
    SCALE = 1.0 / float(np.sqrt(160.0))

    ctx.barrier_all()
    stack = contextlib.ExitStack()
    wdq = salloc(stack, nc, f"{name}_wdq", [128, NC * 32], BF16)
    wdkv = salloc(stack, nc, f"{name}_wdkv", [128, NC * 32], BF16)
    wkr = salloc(stack, nc, f"{name}_wkr", [128, NC * 32], BF16)
    wuq = salloc(stack, nc, f"{name}_wuq", [32, 1024], BF16)
    wuk = salloc(stack, nc, f"{name}_wuk", [32, 1024], BF16)
    wuv = salloc(stack, nc, f"{name}_wuv", [32, 1024], BF16)
    wqr = salloc(stack, nc, f"{name}_wqr", [32, 256], BF16)
    wo = [salloc(stack, nc, f"{name}_wo{i}", [128, NC * 128], BF16) for i in range(2)]
    bdq = salloc(stack, nc, f"{name}_bdq", [32, 3], F32)
    buq = salloc(stack, nc, f"{name}_buq", [128, NC], F32)
    buk = salloc(stack, nc, f"{name}_buk", [128, NC], F32)
    bqr = salloc(stack, nc, f"{name}_bqr", [32, NH], F32)
    bo = salloc(stack, nc, f"{name}_bo", [128, NC], F32)
    buv_b = salloc(stack, nc, f"{name}_buvb", [128, 1024], BF16)
    buv_row = salloc(stack, nc, f"{name}_buvr", [1, 1024], BF16)
    for dst, src in ((wdq, W["mla_WDQ"]), (wdkv, W["mla_WDKV"]), (wkr, W["mla_WKR"])):
        ctx.dma("pool", dst.rearrange("p (k c) -> p k c", k=NC),
                src.rearrange("(k p) c -> p k c", p=128))
    ctx.dma("pool", wuq, W["mla_WUQ"])
    ctx.dma("pool", wuk, W["mla_WUK"])
    ctx.dma("pool", wuv, W["mla_WUV"])
    ctx.dma("pool", wqr, W["mla_WQR"])
    ctx.dma("pool", buv_row, W["mla_bUV"].rearrange("(a c) -> a c", a=1))
    ctx.dma("sp", bdq, W["mla_bsmall"])
    ctx.dma("sp", buq, W["mla_bUQ2"])
    ctx.dma("sp", buk, W["mla_bUK2"])
    ctx.dma("sp", bqr, W["mla_bQR2"])
    dv2 = ctx.dma("sp", bo, W["mla_bO2"])
    ctx.wait("pe", "dma_p", ctx.c("dma_p"))
    ctx.wait("pe", "dma_s", dv2)

    bstack = contextlib.ExitStack()
    psum_b = palloc(bstack, nc, f"{name}_psb", [128, 1024], F32)
    fns = [lambda t, j=j: t.matmul(psum_b[:, j * 512:(j + 1) * 512],
                                   cst.ones[0:1, :], buv_row[:, j * 512:(j + 1) * 512],
                                   start=True, stop=True) for j in range(2)]
    c_b = ctx.ops_last_inc("pe", fns)
    ctx.wait("dve", "pe", c_b)
    ctx.op("dve", lambda v: v.tensor_copy(buv_b[:, :], psum_b[:, :]))
    ctx.barrier_all()
    bstack.close()

    xn_r = xn_fm.rearrange("(c p) t -> p c t", p=128)
    res_in_r = res_in.rearrange("(c p) t -> p c t", p=128)
    res_out_r = res_out.rearrange("(c p) t -> p c t", p=128)

    for seq in range(BC):
        s0 = seq * T
        sstack = contextlib.ExitStack()
        cqs = salloc(sstack, nc, f"{name}{seq}_cqs", [32, 3 * T], BF16)
        qfm = salloc(sstack, nc, f"{name}{seq}_qfm", [128, NC * T], BF16)
        kfm = salloc(sstack, nc, f"{name}{seq}_kfm", [128, NC * T], BF16)
        vtm = salloc(sstack, nc, f"{name}{seq}_vtm", [128, (T // 128) * 1024], BF16)
        qr = salloc(sstack, nc, f"{name}{seq}_qr", [32, NH * T], BF16)
        ss = salloc(sstack, nc, f"{name}{seq}_ss", [128, T], F32)
        ptb = salloc(sstack, nc, f"{name}{seq}_ptb", [128, T], BF16)
        pT = ptb
        osb = salloc(sstack, nc, f"{name}{seq}_osb", [128, 1024], BF16)
        ytile = salloc(sstack, nc, f"{name}{seq}_yt", [128, NC * 128], F32)
        mx = salloc(sstack, nc, f"{name}{seq}_mx", [128, 4], F32)

        xstack = contextlib.ExitStack()
        xnt0 = salloc(xstack, nc, f"{name}{seq}_xnt0", [128, NC * 512], BF16)
        xnt = [xnt0, xnt0]
        ps1 = palloc(xstack, nc, f"{name}{seq}_ps1", [128, 1024], F32)
        ps2 = palloc(xstack, nc, f"{name}{seq}_ps2", [32, 3 * 512], F32)

        ev_done = ctx.c("dve")
        mm_done = {0: ctx.c("pe"), 1: ctx.c("pe")}
        for tc in range(T // 512):
            par = tc % 2
            ctx.wait("sp", "pe", mm_done[par])
            dv = ctx.dma("sp", xnt[par].rearrange("p (c t) -> p c t", c=NC),
                         xn_r[:, :, s0 + tc * 512:s0 + (tc + 1) * 512])
            ctx.wait("pe", "dma_s", dv)
            ctx.wait("pe", "dve", ev_done)
            fns = []
            for j, wv_ in enumerate((wdq, wdkv, wkr)):
                for k in range(NC):
                    fns.append(lambda t, j=j, k=k, wv_=wv_, par=par: t.matmul(
                        ps2[:, j * 512:(j + 1) * 512],
                        wv_[:, k * 32:(k + 1) * 32],
                        xnt[par][:, k * 512:(k + 1) * 512],
                        start=(k == 0), stop=(k == NC - 1)))
            c_mm = ctx.ops_last_inc("pe", fns)
            mm_done[par] = c_mm
            ctx.wait("dve", "pe", c_mm)
            for j in range(3):
                ev_done = ctx.op("dve", lambda v, j=j, tc=tc: v.tensor_scalar_add(
                    cqs[:, j * T + tc * 512:j * T + (tc + 1) * 512],
                    ps2[:, j * 512:(j + 1) * 512], bdq[:, j:j + 1]))

        ev_done = ctx.c("dve")
        for tc in range(T // 512):
            for o in range(NC):
                ctx.wait("pe", "dve", ev_done)
                fns = [
                    lambda t, o=o, tc=tc: t.matmul(
                        ps1[:, 0:512], wuq[:, o * 128:(o + 1) * 128],
                        cqs[:, tc * 512:(tc + 1) * 512], start=True, stop=True),
                    lambda t, o=o, tc=tc: t.matmul(
                        ps1[:, 512:1024], wuk[:, o * 128:(o + 1) * 128],
                        cqs[:, T + tc * 512:T + (tc + 1) * 512], start=True, stop=True),
                ]
                c_mm = ctx.ops_last_inc("pe", fns)
                ctx.wait("dve", "pe", c_mm)
                ctx.op("dve", lambda v, o=o, tc=tc: v.tensor_scalar_add(
                    qfm[:, o * T + tc * 512:o * T + (tc + 1) * 512],
                    ps1[:, 0:512], buq[:, o:o + 1]), inc=False)
                ev_done = ctx.op("dve", lambda v, o=o, tc=tc: v.tensor_scalar_add(
                    kfm[:, o * T + tc * 512:o * T + (tc + 1) * 512],
                    ps1[:, 512:1024], buk[:, o:o + 1]))

        ev_done = ctx.c("dve")
        for tc in range(T // 512):
            for h in range(NH):
                ctx.wait("pe", "dve", ev_done)
                c_mm = ctx.op("pe", lambda t, h=h, tc=tc: t.matmul(
                    ps2[:, 0:512], wqr[:, h * 32:(h + 1) * 32],
                    cqs[:, tc * 512:(tc + 1) * 512], start=True, stop=True))
                ctx.wait("dve", "pe", c_mm)
                ev_done = ctx.op("dve", lambda v, h=h, tc=tc: v.tensor_scalar_add(
                    qr[:, h * T + tc * 512:h * T + (tc + 1) * 512],
                    ps2[:, 0:512], bqr[:, h:h + 1]))

        ev_done = ctx.c("dve")
        for tt in range(T // 128):
            ctx.wait("pe", "dve", ev_done)
            fns = [lambda t, tt=tt, j=j: t.matmul(
                ps1[:, j * 512:(j + 1) * 512],
                cqs[:, T + tt * 128:T + tt * 128 + 128],
                wuv[:, j * 512:(j + 1) * 512], start=True, stop=True)
                for j in range(2)]
            c_mm = ctx.ops_last_inc("pe", fns)
            ctx.wait("dve", "pe", c_mm)
            ev_done = ctx.op("dve", lambda v, tt=tt: v.tensor_add(
                vtm[:, tt * 1024:(tt + 1) * 1024], buv_b[:, :], ps1[:, :]))

        ctx.barrier_all()
        xstack.close()
        ps1 = palloc(sstack, nc, f"{name}{seq}_ps1b", [128, 1024], F32)
        pspt = palloc(sstack, nc, f"{name}{seq}_pspt", [128, 512], BF16)
        pso = palloc(sstack, nc, f"{name}{seq}_pso", [128, 1024], F32)
        psy = palloc(sstack, nc, f"{name}{seq}_psy", [128, 1024], F32)
        if dbg is not None and seq == 0:
            for key, buf in (("cqs", cqs), ("qfm", qfm), ("kfm", kfm),
                             ("vtm", vtm), ("qr", qr)):
                if key in dbg:
                    ctx.dma("sp", dbg[key], buf)
            ctx.barrier_all()
        wo_use = {0: 0, 1: 0}
        c_ss_ev = 0
        c_pt_ev = 0
        c_osb = 0
        c_fin = 0
        dso = 0
        c_mm_o = 0
        for qt in range(T // 128):
            q0 = qt * 128
            for h in range(NH):
                for half in range(2):
                    ctx.wait("pe", "act", c_ss_ev)
                    fns = []
                    for tc in range(2):
                        off = half * 1024 + tc * 512
                        fns.append(lambda t, h=h, off=off, q0=q0, tc=tc: t.matmul(
                            ps1[:, tc * 512:(tc + 1) * 512],
                            qfm[:, h * T + q0:h * T + q0 + 128],
                            kfm[:, h * T + off:h * T + off + 512],
                            start=True, stop=False))
                        fns.append(lambda t, h=h, off=off, q0=q0, tc=tc: t.matmul(
                            ps1[:, tc * 512:(tc + 1) * 512],
                            qr[:, h * T + q0:h * T + q0 + 128],
                            cqs[:, 2 * T + off:2 * T + off + 512],
                            start=False, stop=True))
                    c_mm = ctx.ops_last_inc("pe", fns)
                    ctx.wait("act", "pe", c_mm)
                    ctx.wait("act", "dve", ctx.c("dve"))
                    c_ss_ev = ctx.op("act", lambda s, half=half: s.activation(
                        ss[:, half * 1024:(half + 1) * 1024], ps1[:, :],
                        AF.Copy, scale=SCALE))
                ctx.wait("dve", "act", c_ss_ev)
                ctx.op("dve", lambda v: v.reduce_max(mx[:, 0:1], ss[:, :], axis=AX.X))
                c_neg = ctx.op("dve", lambda v: v.tensor_scalar_mul(
                    mx[:, 1:2], mx[:, 0:1], -1.0))
                ctx.wait("act", "dve", c_neg)
                c_exp = ctx.op("act", lambda s: s.activation(
                    ss[:, :], ss[:, :], AF.Exp, bias=mx[:, 1:2]))
                ctx.wait("dve", "act", c_exp)
                ctx.op("dve", lambda v: v.reduce_sum(mx[:, 2:3], ss[:, :], axis=AX.X))
                ctx.op("dve", lambda v: v.reciprocal(mx[:, 3:4], mx[:, 2:3]))
                c_p = ctx.op("dve", lambda v: v.tensor_scalar_mul(
                    ptb[:, :], ss[:, :], mx[:, 3:4]))
                if dbg is not None and seq == 0 and qt == 0 and h == 0 and "p0" in dbg:
                    cap_p = salloc(sstack, nc, "cap_p", [128, T], BF16)
                    cap_s = salloc(sstack, nc, "cap_s", [128, T], F32)
                    cap_m = salloc(sstack, nc, "cap_m", [128, 4], F32)
                    ctx.op("dve", lambda v: v.tensor_copy(cap_p[:, :], ptb[:, :]), inc=False)
                    ctx.op("dve", lambda v: v.tensor_copy(cap_s[:, :], ss[:, :]), inc=False)
                    c_cap = ctx.op("dve", lambda v: v.tensor_copy(cap_m[:, :], mx[:, :]))
                    ctx.wait("sp", "dve", c_cap)
                    ctx.dma("sp", dbg["p0"], cap_p)
                    ctx.dma("sp", dbg["ss0"], cap_s)
                    ctx.dma("sp", dbg["mx0"], cap_m)
                for rnd in range(4):
                    ctx.wait("pe", "dve", c_p if rnd == 0 else c_pt_ev)
                    fns = []
                    for j in range(4):
                        fns.append(lambda t, rnd=rnd, j=j: t.transpose(
                            pspt[:, j * 128:(j + 1) * 128],
                            ptb[:, rnd * 512 + j * 128:rnd * 512 + (j + 1) * 128],
                            cst.ident[:, :]))
                    c_tr = ctx.ops_last_inc("pe", fns)
                    ctx.wait("dve", "pe", c_tr)
                    c_pt_ev = ctx.op("dve", lambda v, rnd=rnd: v.tensor_copy(
                        pT[:, rnd * 512:(rnd + 1) * 512], pspt[:, :]))
                ctx.wait("pe", "dve", c_pt_ev)
                if h == 0:
                    ctx.wait("pe", "dve", c_osb)
                fns = []
                for tc in range(16):
                    fns.append(lambda t, h=h, tc=tc: t.matmul(
                        pso[:, h * 128:(h + 1) * 128],
                        vtm[:, tc * 1024 + h * 128:tc * 1024 + (h + 1) * 128],
                        pT[:, tc * 128:(tc + 1) * 128],
                        start=(tc == 0 and h % 4 == 0), stop=(tc == 15)))
                c_mm_o = ctx.ops_last_inc("pe", fns)
            ctx.wait("dve", "pe", c_mm_o)
            c_osb = ctx.op("dve", lambda v: v.tensor_copy(osb[:, :], pso[:, :]))
            if dbg is not None and seq == 0 and qt == 0 and "o0" in dbg:
                ctx.wait("sp", "dve", c_osb)
                ctx.dma("sp", dbg["o0"], osb)
            ctx.wait("pe", "dve", c_osb)
            if qt > 0:
                ctx.wait("pe", "dve", c_fin)
            for oc in range(NC):
                wpar = oc % 2
                ctx.wait("pool", "pe", wo_use[wpar])
                dvw = ctx.dma("pool", wo[wpar].rearrange("p (k c) -> p k c", k=NC),
                              W["mla_WO"][:, oc * 128:(oc + 1) * 128].rearrange(
                                  "(k p) c -> p k c", p=128))
                ctx.wait("pe", "dma_p", dvw)
                fns = []
                for k in range(NC):
                    fns.append(lambda t, oc=oc, k=k, wpar=wpar: t.matmul(
                        psy[:, oc * 128:(oc + 1) * 128],
                        wo[wpar][:, k * 128:(k + 1) * 128],
                        osb[:, k * 128:(k + 1) * 128],
                        start=(k == 0 and oc % 4 == 0), stop=(k == NC - 1)))
                c_mm2 = ctx.ops_last_inc("pe", fns)
                wo_use[wpar] = c_mm2
            if qt > 0:
                ctx.wait("sp", "dma_s", dso)
            ctx.wait("sp", "dve", c_fin)
            dvr = ctx.dma("sp", ytile.rearrange("p (c t) -> p c t", c=NC),
                          res_in_r[:, :, s0 + q0:s0 + q0 + 128])
            ctx.wait("dve", "pe", c_mm2)
            ctx.wait("dve", "dma_s", dvr)
            for oc in range(NC):
                ctx.op("dve", lambda v, oc=oc: v.tensor_scalar_add(
                    ytile[:, oc * 128:(oc + 1) * 128],
                    ytile[:, oc * 128:(oc + 1) * 128], bo[:, oc:oc + 1]), inc=False)
                ctx.op("dve", lambda v, oc=oc: v.tensor_add(
                    ytile[:, oc * 128:(oc + 1) * 128],
                    ytile[:, oc * 128:(oc + 1) * 128],
                    psy[:, oc * 128:(oc + 1) * 128]),
                    inc=(oc == NC - 1))
            c_fin = ctx.c("dve")
            ctx.wait("sp", "dve", c_fin)
            dso = ctx.dma("sp", res_out_r[:, :, s0 + q0:s0 + q0 + 128],
                          ytile.rearrange("p (c t) -> p c t", c=NC))
        sstack.close()
        ctx.barrier_all()
    stack.close()
    ctx.barrier_all()


def elu1(ctx, cst, dst_fn, src_ap, tmpA, tmpB):
    ctx.op("dve", lambda v: v.tensor_scalar_max(tmpA, src_ap, 0.0))
    ctx.op("dve", lambda v: v.tensor_scalar_min(tmpB, src_ap, 0.0))
    c_min = ctx.c("dve")
    ctx.wait("act", "dve", c_min)
    c_e = ctx.op("act", lambda s: s.activation(tmpB, tmpB, AF.Exp))
    ctx.wait("dve", "act", c_e)
    return ctx.op("dve", lambda v: dst_fn(v, tmpA, tmpB))


def emit_lattn(ctx, cst, xn_fm, res_in, res_out, W, gamma, beta, hs_d, name="lat"):
    nc = ctx.nc
    EPS_ATTN = 1e-6

    ctx.barrier_all()
    ostack = contextlib.ExitStack()
    bq = salloc(ostack, nc, f"{name}_bq", [128, NC], F32)
    bd = salloc(ostack, nc, f"{name}_bd", [128, NC], F32)
    bkrow = salloc(ostack, nc, f"{name}_bkrow", [1, 2048], BF16)
    bkb = salloc(ostack, nc, f"{name}_bkb", [128, 1024], BF16)
    bvb = salloc(ostack, nc, f"{name}_bvb", [128, 1024], BF16)
    ctx.dma("sp", bq, W["lat_bq2"])
    ctx.dma("sp", bd, W["lat_bd2"])
    dvb = ctx.dma("pool", bkrow, W["lat_bkv"].rearrange("(a c) -> a c", a=1))
    ctx.wait("pe", "dma_p", dvb)
    bstack = contextlib.ExitStack()
    psb = palloc(bstack, nc, f"{name}_psb", [128, 1024], F32)
    for half, dstb in ((0, bkb), (1, bvb)):
        fns = [lambda t, j=j, half=half: t.matmul(
            psb[:, j * 512:(j + 1) * 512], cst.ones[0:1, :],
            bkrow[:, half * 1024 + j * 512:half * 1024 + (j + 1) * 512],
            start=True, stop=True) for j in range(2)]
        c_b = ctx.ops_last_inc("pe", fns)
        ctx.wait("dve", "pe", c_b)
        c_c = ctx.op("dve", lambda v, dstb=dstb: v.tensor_copy(dstb[:, :], psb[:, :]))
        ctx.wait("pe", "dve", c_c)
    ctx.barrier_all()
    bstack.close()

    xn_r = xn_fm.rearrange("(c p) t -> p c t", p=128)
    hs_r = hs_d.rearrange("(c p) t -> p c t", p=128)

    for seq in range(BC):
        s0 = seq * T
        sstack = contextlib.ExitStack()
        qfm = salloc(sstack, nc, f"{name}{seq}_qfm", [128, NC * T], BF16)
        etmp = salloc(sstack, nc, f"{name}{seq}_et", [128, 2048], F32)
        ksum = salloc(sstack, nc, f"{name}{seq}_ks", [128, 8], F32)
        kbd = salloc(sstack, nc, f"{name}{seq}_kbd", [128, NC * 16], BF16)
        zin = salloc(sstack, nc, f"{name}{seq}_zin", [16, T], BF16)
        kvsb = salloc(sstack, nc, f"{name}{seq}_kv", [128, NC * 128], BF16)
        ps1 = palloc(sstack, nc, f"{name}{seq}_ps1", [128, 1024], F32)
        ps2 = palloc(sstack, nc, f"{name}{seq}_ps2", [128, 1024], F32)
        psz = palloc(sstack, nc, f"{name}{seq}_psz", [16, 512], F32)
        kvstack = contextlib.ExitStack()
        ktm = salloc(kvstack, nc, f"{name}{seq}_ktm", [128, (T // 128) * 1024], BF16)
        vtm = salloc(kvstack, nc, f"{name}{seq}_vtm", [128, (T // 128) * 1024], BF16)

        astack = contextlib.ExitStack()
        xtl0 = salloc(astack, nc, f"{name}{seq}_xtl0", [128, NC * 512], BF16)
        xtl = [xtl0, xtl0]
        wres = salloc(astack, nc, f"{name}{seq}_wres", [128, NC * 1024], BF16)
        wpan = [salloc(astack, nc, f"{name}{seq}_wp{i}", [128, NC * 128], BF16)
                for i in range(2)]

        pan_use = {0: 0, 1: 0}
        xt_use = {0: 0, 1: 0}
        ev = ctx.c("dve")
        for tc in range(T // 512):
            xpar = tc % 2
            ctx.wait("sp", "pe", xt_use[xpar])
            dvx = ctx.dma("sp", xtl[xpar].rearrange("p (c t) -> p c t", c=NC),
                          xn_r[:, :, s0 + tc * 512:s0 + (tc + 1) * 512])
            ctx.wait("pe", "dma_s", dvx)
            for o in range(NC):
                par = o % 2
                ctx.wait("pool", "pe", pan_use[par])
                dvw = ctx.dma("pool", wpan[par].rearrange("p (k c) -> p k c", k=NC),
                              W["lat_WqT"][:, o * 128:(o + 1) * 128].rearrange(
                                  "(k p) c -> p k c", p=128))
                ctx.wait("pe", "dma_p", dvw)
                reg = o % 2
                ctx.wait("pe", "dve", ev)
                fns = []
                for k in range(NC):
                    fns.append(lambda t, k=k, par=par, reg=reg, xpar=xpar: t.matmul(
                        ps1[:, reg * 512:(reg + 1) * 512],
                        wpan[par][:, k * 128:(k + 1) * 128],
                        xtl[xpar][:, k * 512:(k + 1) * 512],
                        start=(k == 0), stop=(k == NC - 1)))
                c_mm = ctx.ops_last_inc("pe", fns)
                pan_use[par] = c_mm
                xt_use[xpar] = c_mm
                ctx.wait("dve", "pe", c_mm)
                ctx.op("dve", lambda v, o=o, reg=reg: v.tensor_scalar_add(
                    etmp[:, 0:512], ps1[:, reg * 512:(reg + 1) * 512], bq[:, o:o + 1]))
                def qdst(v, a, b, o=o, tc=tc):
                    return v.tensor_add(
                        qfm[:, o * T + tc * 512:o * T + (tc + 1) * 512], a, b)
                ev = elu1(ctx, cst, qdst, etmp[:, 0:512],
                          etmp[:, 512:1024], etmp[:, 1024:1536])
        ev = ctx.c("dve")
        for which in range(2):
            dvw = ctx.dma("pool", wres.rearrange("p (k c) -> p k c", k=NC),
                          (W["lat_WkT"] if which == 0 else W["lat_WvT"]).rearrange(
                              "(k p) c -> p k c", p=128))
            ctx.wait("pe", "dma_p", dvw)
            for tt in range(T // 128):
                xpar = tt % 2
                ctx.wait("sp", "pe", xt_use[xpar])
                dvx = ctx.dma("sp",
                              xtl[xpar][:, 0:NC * 128].rearrange("p (c t) -> p c t", c=NC),
                              xn_r[:, :, s0 + tt * 128:s0 + (tt + 1) * 128])
                ctx.wait("pe", "dma_s", dvx)
                ctx.wait("pe", "dve", ev)
                fns = []
                for k in range(NC):
                    for j in range(2):
                        fns.append(lambda t, k=k, j=j, xpar=xpar: t.matmul(
                            ps2[:, j * 512:(j + 1) * 512],
                            xtl[xpar][:, k * 128:(k + 1) * 128],
                            wres[:, k * 1024 + j * 512:k * 1024 + (j + 1) * 512],
                            start=(k == 0), stop=(k == NC - 1)))
                c_mm = ctx.ops_last_inc("pe", fns)
                xt_use[xpar] = c_mm
                ctx.wait("dve", "pe", c_mm)
                if which == 0:
                    ctx.op("dve", lambda v: v.tensor_add(
                        etmp[:, 0:1024], ps2[:, :], bkb[:, :]))
                    def kdst(v, a, b, tt=tt):
                        return v.tensor_add(ktm[:, tt * 1024:(tt + 1) * 1024], a, b)
                    ev = elu1(ctx, cst, kdst, etmp[:, 0:1024],
                              etmp[:, 1024:2048], etmp[:, 0:1024])
                else:
                    ev = ctx.op("dve", lambda v, tt=tt: v.tensor_add(
                        vtm[:, tt * 1024:(tt + 1) * 1024], ps2[:, :], bvb[:, :]))
        ctx.barrier_all()
        astack.close()

        c64stack = contextlib.ExitStack()
        ctx64 = salloc(c64stack, nc, f"{name}{seq}_c64", [64, 16 * T], BF16)
        fns = []
        for c in range(NC):
            for tt in range(T // 128):
                fns.append(lambda t, c=c, tt=tt: t.matmul(
                    ps1[:, c:c + 1],
                    ktm[:, tt * 1024 + c * 128:tt * 1024 + (c + 1) * 128],
                    cst.ones[:, 0:1],
                    start=(c == 0 and tt == 0), stop=(tt == T // 128 - 1)))
        c_mm = ctx.ops_last_inc("pe", fns)
        ctx.wait("dve", "pe", c_mm)
        ctx.op("dve", lambda v: v.tensor_copy(ksum[:, :], ps1[:, 0:8]))
        ctx.op("dve", lambda v: v.memset(kbd[:, :], 0.0))
        for h in range(16):
            c = h // 2
            r0 = (h % 2) * 64
            ctx.op("dve", lambda v, h=h, c=c, r0=r0: v.tensor_scalar_add(
                kbd[r0:r0 + 64, c * 16 + h:c * 16 + h + 1],
                ksum[r0:r0 + 64, c:c + 1], EPS_ATTN), inc=False)
        c_kbd = ctx.op("dve", lambda v: v.engine_nop())
        ctx.wait("pe", "dve", c_kbd)
        ev = ctx.c("dve")
        for tc in range(T // 512):
            ctx.wait("pe", "dve", ev)
            fns = []
            for k in range(NC):
                fns.append(lambda t, k=k, tc=tc: t.matmul(
                    psz[:, :], kbd[:, k * 16:(k + 1) * 16],
                    qfm[:, k * T + tc * 512:k * T + (tc + 1) * 512],
                    start=(k == 0), stop=(k == NC - 1)))
            c_mm = ctx.ops_last_inc("pe", fns)
            ctx.wait("dve", "pe", c_mm)
            ctx.op("dve", lambda v: v.reciprocal(etmp[0:16, 0:512], psz[:, :]))
            ev = ctx.op("dve", lambda v, tc=tc: v.tensor_copy(
                zin[:, tc * 512:(tc + 1) * 512], etmp[0:16, 0:512]))
        ctx.wait("pe", "dve", ev)
        fns = []
        for c in range(NC):
            for tt in range(T // 128):
                fns.append(lambda t, c=c, tt=tt: t.matmul(
                    ps2[:, c * 128:(c + 1) * 128],
                    ktm[:, tt * 1024 + c * 128:tt * 1024 + (c + 1) * 128],
                    vtm[:, tt * 1024 + c * 128:tt * 1024 + (c + 1) * 128],
                    start=(tt == 0 and c % 4 == 0), stop=(tt == T // 128 - 1)))
        c_mm = ctx.ops_last_inc("pe", fns)
        ctx.wait("dve", "pe", c_mm)
        c_kv = ctx.op("dve", lambda v: v.tensor_copy(kvsb[:, :], ps2[:, :]))
        ctx.wait("pe", "dve", c_kv)
        ev = ctx.c("dve")
        for h in range(16):
            c = h // 2
            j = h % 2
            for tc in range(T // 512):
                ctx.wait("pe", "dve", ev)
                c_zb = ctx.op("pe", lambda t, h=h, tc=tc: t.matmul(
                    ps1[0:64, 0:512], cst.selz64[:, h * 64:(h + 1) * 64],
                    zin[:, tc * 512:(tc + 1) * 512], start=True, stop=True))
                ctx.wait("dve", "pe", c_zb)
                c_ev = ctx.op("dve", lambda v: v.tensor_copy(
                    etmp[0:64, 0:512], ps1[0:64, 0:512]))
                c_cm = ctx.op("pe", lambda t, c=c, j=j, tc=tc: t.matmul(
                    ps1[0:64, 512:1024],
                    kvsb[j * 64:(j + 1) * 64,
                         c * 128 + j * 64:c * 128 + (j + 1) * 64],
                    qfm[j * 64:(j + 1) * 64, c * T + tc * 512:c * T + (tc + 1) * 512],
                    start=True, stop=True))
                ctx.wait("dve", "pe", c_cm)
                ev = ctx.op("dve", lambda v, h=h, tc=tc: v.tensor_mul(
                    ctx64[:, h * T + tc * 512:h * T + (tc + 1) * 512],
                    ps1[0:64, 512:1024], etmp[0:64, 0:512]))
        hstack = contextlib.ExitStack()
        wpan2 = [salloc(hstack, nc, f"{name}{seq}_wq{i}", [64, 16 * 128], BF16)
                 for i in range(2)]
        xtile = salloc(hstack, nc, f"{name}{seq}_xt2", [128, 512], BF16)
        htile = salloc(hstack, nc, f"{name}{seq}_hs", [128, 512], F32)
        pan_use2 = {0: 0, 1: 0}
        ev = ctx.c("dve")
        dxu = 0
        for o in range(NC):
            par = o % 2
            ctx.wait("pool", "pe", pan_use2[par])
            dvw = ctx.dma("pool", wpan2[par].rearrange("p (h c) -> p h c", h=16),
                          W["lat_WdT"][:, o * 128:(o + 1) * 128].rearrange(
                              "(h p) c -> p h c", p=64))
            ctx.wait("pe", "dma_p", dvw)
            for tc in range(T // 512):
                reg = tc % 2
                ctx.wait("sp", "dve", ctx.c("dve"))
                ctx.wait("sp", "dma_s", dxu)
                dxu = ctx.dma("sp", xtile[:, :],
                              xn_r[:, o, s0 + tc * 512:s0 + (tc + 1) * 512])
                ctx.wait("pe", "dve", ev)
                fns = []
                for h in range(16):
                    fns.append(lambda t, h=h, tc=tc, par=par, reg=reg: t.matmul(
                        ps1[:, reg * 512:(reg + 1) * 512],
                        wpan2[par][:, h * 128:(h + 1) * 128],
                        ctx64[:, h * T + tc * 512:h * T + (tc + 1) * 512],
                        start=(h == 0), stop=(h == 15)))
                c_mm = ctx.ops_last_inc("pe", fns)
                pan_use2[par] = c_mm
                ctx.wait("dve", "pe", c_mm)
                ctx.wait("dve", "dma_s", dxu)
                ctx.op("dve", lambda v, o=o, reg=reg: v.tensor_scalar_add(
                    htile[:, :], ps1[:, reg * 512:(reg + 1) * 512], bd[:, o:o + 1]))
                ev = ctx.op("dve", lambda v: v.tensor_add(
                    htile[:, :], htile[:, :], xtile[:, :]))
                ctx.wait("sp", "dve", ev)
                ctx.dma("sp", hs_r[:, o, s0 + tc * 512:s0 + (tc + 1) * 512],
                        htile[:, :])
        ctx.barrier_all()
        hstack.close()
        c64stack.close()
        kvstack.close()
        sstack.close()
    ostack.close()
    emit_ln(ctx, cst, hs_d, None, gamma=gamma, beta=beta,
            eps=1e-12, res_in=res_in, res_out=res_out, name=name + "fln")


def emit_mamba(ctx, cst, xn2_fm, xn1_fm, res_in, res_out, W, S, name="mam", upto=99):
    if upto <= -1:
        return
    nc = ctx.nc
    DI = 2048
    NDC = DI // 128
    NS = 16
    TQ = 512
    xz_r = S["xz_d"].rearrange("(c p) t -> p c t", p=128)
    xc_r = S["xc_d"].rearrange("(c p) t -> p c t", p=128)
    dt_r = S["dt_d"].rearrange("(c p) t -> p c t", p=128)
    dtu_r = S["dtu_d"].rearrange("(c p) t -> p c t", p=128)
    y_r = S["y_d"].rearrange("(c p) t -> p c t", p=128)
    xn_r = xn2_fm.rearrange("(c p) t -> p c t", p=128)
    xn1_r = xn1_fm.rearrange("(c p) t -> p c t", p=128)
    res_in_r = res_in.rearrange("(c p) t -> p c t", p=128)
    res_out_r = res_out.rearrange("(c p) t -> p c t", p=128)

    ctx.barrier_all()
    ostack = contextlib.ExitStack()
    convw = salloc(ostack, nc, f"{name}_cw", [128, NDC * 4], F32)
    convb = salloc(ostack, nc, f"{name}_cb", [128, NDC], F32)
    bdt = salloc(ostack, nc, f"{name}_sbdt", [128, NDC], F32)
    Asb = salloc(ostack, nc, f"{name}_A", [128, NDC * NS], F32)
    Dcol = salloc(ostack, nc, f"{name}_sD", [128, NDC], F32)
    ctx.dma("sp", convw, W["mam_convw"])
    ctx.dma("sp", convb, W["mam_convb"])
    ctx.dma("sp", bdt, W["mam_bdt"])
    dva = ctx.dma("sp", Asb, W["mam_Alog"])
    ctx.dma("sp", Dcol, W["mam_D"])
    ctx.wait("act", "dma_s", dva)
    c_a = ctx.op("act", lambda s: s.activation(Asb[:, :], Asb[:, :], AF.Exp))
    ctx.wait("dve", "act", c_a)
    c_a2 = ctx.op("dve", lambda v: v.tensor_scalar_mul(Asb[:, :], Asb[:, :], -1.0))
    ctx.wait("act", "dve", c_a2)

    if upto <= 0:
        return
    m1 = contextlib.ExitStack()
    xnt = salloc(m1, nc, f"{name}_xnt", [128, NC * NT], BF16)
    wpan = [salloc(m1, nc, f"{name}_wp{i}", [128, NC * 128], BF16) for i in range(2)]
    otile = [salloc(m1, nc, f"{name}_ot{i}", [128, 4 * 512], BF16) for i in range(2)]
    psA = palloc(m1, nc, f"{name}_psA", [128, 4 * 512], F32)
    psB = palloc(m1, nc, f"{name}_psB", [128, 4 * 512], F32)
    dv = ctx.dma("sp", xnt.rearrange("p (c t) -> p c t", c=NC), xn_r[:, :, :])
    ctx.wait("pe", "dma_s", dv)
    pan_use = {0: 0, 1: 0}
    ev_done = {0: 0, 1: 0}
    st_use = {0: 0, 1: 0}
    for o in range(32):
        par = o % 2
        psX = psA if par == 0 else psB
        ctx.wait("pool", "pe", pan_use[par])
        dvw = ctx.dma("pool", wpan[par].rearrange("p (k c) -> p k c", k=NC),
                      W["mam_Win"][:, o * 128:(o + 1) * 128].rearrange(
                          "(k p) c -> p k c", p=128))
        ctx.wait("pe", "dma_p", dvw)
        for half in range(2):
            ctx.wait("pe", "dve", ev_done[par])
            fns = []
            for k in range(NC):
                for t4 in range(4):
                    fns.append(lambda t, k=k, t4=t4, half=half, par=par, psX=psX: t.matmul(
                        psX[:, t4 * 512:(t4 + 1) * 512],
                        wpan[par][:, k * 128:(k + 1) * 128],
                        xnt[:, k * NT + half * 2048 + t4 * 512:k * NT + half * 2048 + (t4 + 1) * 512],
                        start=(k == 0), stop=(k == NC - 1)))
            c_mm = ctx.ops_last_inc("pe", fns)
            pan_use[par] = c_mm
            ctx.wait("dve", "pe", c_mm)
            ctx.wait("dve", "dma_s", st_use[par])
            c_o = ctx.op("dve", lambda v, par=par, psX=psX: v.tensor_copy(
                otile[par][:, :], psX[:, :]))
            ev_done[par] = c_o
            ctx.wait("sp", "dve", c_o)
            st_use[par] = ctx.dma("sp", xz_r[:, o, half * 2048:(half + 1) * 2048],
                                  otile[par][:, 0:2048])
    ctx.barrier_all()
    m1.close()
    if upto <= 1:
        return

    m2 = contextlib.ExitStack()
    xt = salloc(m2, nc, f"{name}_m2x", [128, T], BF16)
    acc = salloc(m2, nc, f"{name}_m2a", [128, T], F32)
    xo = salloc(m2, nc, f"{name}_m2o", [128, T], BF16)
    st_u = 0
    for seq in range(BC):
        for dc in range(NDC):
            ctx.wait("sp", "dve", ctx.c("dve"))
            ctx.wait("sp", "act", ctx.c("act"))
            dv = ctx.dma("sp", xt[:, :], xz_r[:, dc, seq * T:(seq + 1) * T])
            ctx.wait("dve", "dma_s", dv)
            ctx.op("dve", lambda v, dc=dc: v.tensor_scalar_mul(
                acc[:, :], xt[:, :], convw[:, dc * 4 + 3:dc * 4 + 4]))
            for j in (2, 1, 0):
                sh = 3 - j
                ctx.op("dve", lambda v, dc=dc, j=j, sh=sh: v.scalar_tensor_tensor(
                    acc[:, sh:T], xt[:, 0:T - sh], convw[:, dc * 4 + j:dc * 4 + j + 1],
                    acc[:, sh:T], ALU.mult, ALU.add))
            c_acc = ctx.c("dve")
            ctx.wait("act", "dve", c_acc)
            c_sil = ctx.op("act", lambda s, dc=dc: s.activation(
                xo[:, :], acc[:, :], AF.Silu, bias=convb[:, dc:dc + 1]))
            ctx.wait("sp", "act", c_sil)
            ctx.wait("sp", "dma_s", st_u)
            st_u = ctx.dma("sp", xc_r[:, dc, seq * T:(seq + 1) * T], xo[:, :])
    ctx.barrier_all()
    m2.close()
    if upto <= 2:
        return

    for seq in range(BC):
        s0 = seq * T
        m3 = contextlib.ExitStack()
        xdbl = salloc(m3, nc, f"{name}{seq}_xdbl", [96, T], BF16)
        wxp = salloc(m3, nc, f"{name}{seq}_wxp", [128, NDC * 96], BF16)
        wdt = salloc(m3, nc, f"{name}{seq}_wdt", [64, DI], BF16)
        xct = salloc(m3, nc, f"{name}{seq}_xct", [128, T], BF16)
        dtt = salloc(m3, nc, f"{name}{seq}_dtt", [128, T], F32)
        dut = salloc(m3, nc, f"{name}{seq}_dut", [128, T], F32)
        sp1 = salloc(m3, nc, f"{name}{seq}_sp1", [128, 512], F32)
        sp2 = salloc(m3, nc, f"{name}{seq}_sp2", [128, 512], F32)
        ps = palloc(m3, nc, f"{name}{seq}_ps", [96, 4 * 512], F32)
        ps2 = palloc(m3, nc, f"{name}{seq}_ps2", [128, 512], F32)
        dvw = ctx.dma("pool", wxp.rearrange("p (k c) -> p k c", k=NDC),
                      W["mam_Wxp"].rearrange("(k p) c -> p k c", p=128))
        dvw2 = ctx.dma("pool", wdt, W["mam_Wdt"])
        ctx.wait("pe", "dma_p", dvw2)
        dxc = 0
        for dc in range(NDC):
            ctx.wait("sp", "pe", ctx.c("pe"))
            dxc = ctx.dma("sp", xct[:, :], xc_r[:, dc, s0:s0 + T])
            ctx.wait("pe", "dma_s", dxc)
            fns = []
            for tc in range(4):
                fns.append(lambda t, dc=dc, tc=tc: t.matmul(
                    ps[:, tc * 512:(tc + 1) * 512],
                    wxp[:, dc * 96:(dc + 1) * 96],
                    xct[:, tc * 512:(tc + 1) * 512],
                    start=(dc == 0), stop=(dc == NDC - 1)))
            c_mm = ctx.ops_last_inc("pe", fns)
        ctx.wait("dve", "pe", c_mm)
        c_xd = ctx.op("dve", lambda v: v.tensor_copy(xdbl[:, :], ps[:, 0:2048]))
        ctx.wait("pe", "dve", c_xd)
        ctx.wait("sp", "dve", c_xd)
        ctx.dma("sp", S["xdbl_d"][:, s0:s0 + T], xdbl[:, :])
        if upto <= 3:
            ctx.barrier_all()
            m3.close()
            continue
        st1 = st2 = 0
        for dc in range(NDC):
            ctx.wait("sp", "dve", ctx.c("dve"))
            ctx.wait("dve", "dma_s", max(st1, st2))
            dxc = ctx.dma("sp", xct[:, :], xc_r[:, dc, s0:s0 + T])
            for tc in range(4):
                ctx.wait("pe", "act", ctx.c("act"))
                ctx.wait("pe", "dve", ctx.c("dve"))
                c_mm = ctx.op("pe", lambda t, dc=dc, tc=tc: t.matmul(
                    ps2[:, :], wdt[:, dc * 128:(dc + 1) * 128],
                    xdbl[0:64, tc * 512:(tc + 1) * 512], start=True, stop=True))
                ctx.wait("act", "pe", c_mm)
                c_abs = ctx.op("act", lambda s, dc=dc: s.activation(
                    sp1[:, :], ps2[:, :], AF.Abs, bias=bdt[:, dc:dc + 1]))
                ctx.op("act", lambda s: s.activation(sp1[:, :], sp1[:, :], AF.Exp, scale=-1.0))
                c_l = ctx.op("act", lambda s: s.activation(
                    sp1[:, :], sp1[:, :], AF.Ln, bias=cst.fscal[:, FS[1.0]:FS[1.0] + 1]))
                ctx.wait("dve", "pe", c_mm)
                ctx.wait("dve", "act", c_abs)
                ctx.op("dve", lambda v, dc=dc: v.tensor_scalar(
                    sp2[:, :], ps2[:, :], bdt[:, dc:dc + 1], 0.0, ALU.add, ALU.max))
                ctx.wait("dve", "act", c_l)
                ctx.wait("dve", "dma_s", dxc)
                ctx.op("dve", lambda v, tc=tc: v.tensor_add(
                    dtt[:, tc * 512:(tc + 1) * 512], sp2[:, :], sp1[:, :]))
                ctx.op("dve", lambda v, tc=tc: v.tensor_mul(
                    dut[:, tc * 512:(tc + 1) * 512],
                    dtt[:, tc * 512:(tc + 1) * 512], xct[:, tc * 512:(tc + 1) * 512]))
            c_fin = ctx.c("dve")
            ctx.wait("sp", "dve", c_fin)
            ctx.wait("sp", "dma_s", max(st1, st2))
            st1 = ctx.dma("sp", dt_r[:, dc, s0:s0 + T], dtt[:, :])
            st2 = ctx.dma("sp", dtu_r[:, dc, s0:s0 + T], dut[:, :])
        ctx.barrier_all()
        m3.close()
        if upto <= 4:
            continue

        m5 = contextlib.ExitStack()
        xdbl2 = salloc(m5, nc, f"{name}{seq}_xdbl2", [96, T], BF16)
        Brep = salloc(m5, nc, f"{name}{seq}_Brep", [128, NS * TQ], F32)
        Crep = salloc(m5, nc, f"{name}{seq}_Crep", [128, NS * TQ], BF16)
        hprod = salloc(m5, nc, f"{name}{seq}_hp", [128, NS * TQ], BF16)
        datile = salloc(m5, nc, f"{name}{seq}_da", [128, TQ], F32)
        xtile = salloc(m5, nc, f"{name}{seq}_xi", [128, TQ], F32)
        dtq = salloc(m5, nc, f"{name}{seq}_dtq", [128, TQ], F32)
        duq = salloc(m5, nc, f"{name}{seq}_duq", [128, TQ], F32)
        xcq = salloc(m5, nc, f"{name}{seq}_xcq", [128, TQ], BF16)
        zq = salloc(m5, nc, f"{name}{seq}_zq", [128, TQ], BF16)
        yq = salloc(m5, nc, f"{name}{seq}_yq", [128, TQ], F32)
        yo = salloc(m5, nc, f"{name}{seq}_yo", [128, TQ], BF16)
        state = salloc(m5, nc, f"{name}{seq}_st", [128, NDC * NS], F32)
        psq = palloc(m5, nc, f"{name}{seq}_psq", [128, 512], F32)
        dxb = ctx.dma("sp", xdbl2[:, :], S["xdbl_d"][:, s0:s0 + T])
        c_z = ctx.op("dve", lambda v: v.memset(state[:, :], 0.0))
        ctx.wait("dve", "dma_s", dxb)
        for q in range(T // TQ):
            t0 = s0 + q * TQ
            ev = ctx.c("dve")
            for n in range(NS):
                ctx.wait("pe", "dve", ev)
                c_b = ctx.op("pe", lambda t, n=n, q=q: t.matmul(
                    psq[:, :], cst.seln[:, n * 128:(n + 1) * 128],
                    xdbl2[:, q * TQ:(q + 1) * TQ], start=True, stop=True))
                ctx.wait("dve", "pe", c_b)
                ev = ctx.op("dve", lambda v, n=n: v.tensor_copy(
                    Brep[:, n * TQ:(n + 1) * TQ], psq[:, :]))
                ctx.wait("pe", "dve", ev)
                c_c = ctx.op("pe", lambda t, n=n, q=q: t.matmul(
                    psq[:, :], cst.selc[:, n * 128:(n + 1) * 128],
                    xdbl2[:, q * TQ:(q + 1) * TQ], start=True, stop=True))
                ctx.wait("dve", "pe", c_c)
                ev = ctx.op("dve", lambda v, n=n: v.tensor_copy(
                    Crep[:, n * TQ:(n + 1) * TQ], psq[:, :]))
            for dc in range(NDC):
                ctx.wait("sp", "dve", ctx.c("dve"))
                ctx.wait("sp", "act", ctx.c("act"))
                ctx.dma("sp", dtq[:, :], dt_r[:, dc, t0:t0 + TQ])
                ctx.dma("sp", duq[:, :], dtu_r[:, dc, t0:t0 + TQ])
                ctx.dma("sp", xcq[:, :], xc_r[:, dc, t0:t0 + TQ])
                dvz = ctx.dma("sp", zq[:, :], xz_r[:, NDC + dc, t0:t0 + TQ])
                ctx.wait("dve", "dma_s", dvz)
                ctx.wait("act", "dma_s", dvz)
                for n in range(NS):
                    ctx.wait("act", "dve", ctx.c("dve"))
                    c_da = ctx.op("act", lambda s, dc=dc, n=n: s.activation(
                        datile[:, :], dtq[:, :], AF.Exp,
                        scale=Asb[:, dc * NS + n:dc * NS + n + 1]))
                    ctx.op("dve", lambda v, n=n: v.tensor_mul(
                        xtile[:, :], duq[:, :], Brep[:, n * TQ:(n + 1) * TQ]))
                    ctx.wait("dve", "act", c_da)
                    ctx.op("dve", lambda v, n=n, dc=dc: v.tensor_tensor_scan(
                        hprod[:, n * TQ:(n + 1) * TQ], datile[:, :], xtile[:, :],
                        state[:, dc * NS + n:dc * NS + n + 1], ALU.mult, ALU.add))
                    ctx.op("dve", lambda v, n=n, dc=dc: v.tensor_copy(
                        state[:, dc * NS + n:dc * NS + n + 1],
                        hprod[:, (n + 1) * TQ - 1:(n + 1) * TQ]))
                ctx.op("dve", lambda v: v.tensor_mul(hprod[:, :], hprod[:, :], Crep[:, :]))
                ctx.op("dve", lambda v: v.tensor_reduce(
                    yq.rearrange("p (t u) -> p t u", u=1),
                    hprod.rearrange("p (n t) -> p t n", n=NS),
                    AX.X, ALU.add))
                ctx.op("dve", lambda v, dc=dc: v.scalar_tensor_tensor(
                    yq[:, :], xcq[:, :], Dcol[:, dc:dc + 1], yq[:, :],
                    ALU.mult, ALU.add))
                ctx.wait("act", "dve", ctx.c("dve"))
                c_sz = ctx.op("act", lambda s: s.activation(xtile[:, :], zq[:, :], AF.Silu))
                ctx.wait("dve", "act", c_sz)
                c_y = ctx.op("dve", lambda v: v.tensor_mul(yo[:, :], yq[:, :], xtile[:, :]))
                ctx.wait("sp", "dve", c_y)
                ctx.dma("sp", y_r[:, dc, t0:t0 + TQ], yo[:, :])
        ctx.barrier_all()
        m5.close()
    if upto <= 5:
        return

    m6 = contextlib.ExitStack()
    ybuf = salloc(m6, nc, f"{name}_ybuf", [128, NDC * 2048], BF16)
    xn1t = [salloc(m6, nc, f"{name}_x1t{i}", [128, 4 * 512], BF16) for i in range(2)]
    wpan6 = [salloc(m6, nc, f"{name}_w6{i}", [128, NDC * 128], BF16) for i in range(2)]
    rt6 = [salloc(m6, nc, f"{name}_rt6{i}", [128, 4 * 512], F32) for i in range(2)]
    psA6 = palloc(m6, nc, f"{name}_psA6", [128, 4 * 512], F32)
    psB6 = palloc(m6, nc, f"{name}_psB6", [128, 4 * 512], F32)
    pan_use = {0: 0, 1: 0}
    ev_done = {0: 0, 1: 0}
    rt_use = {0: 0, 1: 0}
    for half in range(2):
        t0 = half * 2048
        ctx.wait("sp", "pe", ctx.c("pe"))
        dv = ctx.dma("sp", ybuf.rearrange("p (c t) -> p c t", c=NDC),
                     y_r[:, :, t0:t0 + 2048])
        ctx.wait("pe", "dma_s", dv)
        for o in range(NC):
            par = o % 2
            psX = psA6 if par == 0 else psB6
            ctx.wait("pool", "pe", pan_use[par])
            dvw = ctx.dma("pool", wpan6[par].rearrange("p (k c) -> p k c", k=NDC),
                          W["mam_Wout"][:, o * 128:(o + 1) * 128].rearrange(
                              "(k p) c -> p k c", p=128))
            ctx.wait("sp", "dma_s", rt_use[par])
            ctx.wait("sp", "dve", ctx.c("dve"))
            dvr = ctx.dma("sp", rt6[par][:, :], res_in_r[:, o, t0:t0 + 2048])
            dvx1 = ctx.dma("sp", xn1t[par][:, :], xn1_r[:, o, t0:t0 + 2048])
            ctx.wait("pe", "dma_p", dvw)
            ctx.wait("pe", "dve", ev_done[par])
            fns = []
            for k in range(NDC):
                for t4 in range(4):
                    fns.append(lambda t, k=k, t4=t4, psX=psX, par=par: t.matmul(
                        psX[:, t4 * 512:(t4 + 1) * 512],
                        wpan6[par][:, k * 128:(k + 1) * 128],
                        ybuf[:, k * 2048 + t4 * 512:k * 2048 + (t4 + 1) * 512],
                        start=(k == 0), stop=(k == NDC - 1)))
            c_mm = ctx.ops_last_inc("pe", fns)
            pan_use[par] = c_mm
            ctx.wait("dve", "pe", c_mm)
            ctx.wait("dve", "dma_s", dvx1)
            ctx.op("dve", lambda v, par=par: v.tensor_add(
                rt6[par][:, :], rt6[par][:, :], xn1t[par][:, :]))
            c_o = ctx.op("dve", lambda v, par=par, psX=psX: v.tensor_add(
                rt6[par][:, :], rt6[par][:, :], psX[:, :]))
            ev_done[par] = c_o
            ctx.wait("sp", "dve", c_o)
            rt_use[par] = ctx.dma("sp", res_out_r[:, o, t0:t0 + 2048], rt6[par][:, :])
    ctx.barrier_all()
    m6.close()
    ostack.close()


def emit_rwkv(ctx, cst, xn_fm, res_in, res_out, W, S, name="rw", only_p3=False):
    nc = ctx.nc
    C = 64
    NCH = T // C
    xn_r = xn_fm.rearrange("(c p) t -> p c t", p=128) if xn_fm is not None else None
    res_in_r = res_in.rearrange("(c p) t -> p c t", p=128)
    res_out_r = res_out.rearrange("(c p) t -> p c t", p=128)
    mix_r = [S[f"mix{f}_d"].rearrange("(c p) t -> p c t", p=128) for f in range(5)] if not only_p3 else None
    dx_r = S["dx_d"].rearrange("(c p) t -> p c t", p=128) if not only_p3 else None
    r_r = S["r_d"].rearrange("(h p) t -> p h t", p=64) if not only_p3 else None
    k_r = S["k_d"].rearrange("(h p) t -> p h t", p=64) if not only_p3 else None
    g_r = S["g_d"].rearrange("(h p) t -> p h t", p=64)
    gate_r = S["gate_d"].rearrange("(h p) t -> p h t", p=64)
    y_r = S["y_d"].rearrange("(h p) t -> p h t", p=64)

    ctx.barrier_all()
    ostack = contextlib.ExitStack()
    maa = salloc(ostack, nc, f"{name}_smaa", [128, NC * 6], F32)
    cw = salloc(ostack, nc, f"{name}_cwv", [128, NC * 3], F32)
    gateb = salloc(ostack, nc, f"{name}_gb", [64, 16], F32)
    lnw = salloc(ostack, nc, f"{name}_lnwv", [64, 16], F32)
    lnb = salloc(ostack, nc, f"{name}_lnbv", [64, 16], F32)
    ctx.dma("sp", maa, W["rw_maa"])
    ctx.dma("sp", cw, W["rw_convw"])
    ctx.dma("sp", gateb, W["rw_gateb64"])
    ctx.dma("sp", lnw, W["rw_lnw64"])
    dvc = ctx.dma("sp", lnb, W["rw_lnb64"])
    ctx.wait("act", "dma_s", dvc)
    ctx.wait("dve", "dma_s", dvc)

    for seq in range(BC):
        s0 = seq * T
        if not only_p3:
            p0 = contextlib.ExitStack()
            xns = salloc(p0, nc, f"{name}{seq}_xn", [128, NC * T], BF16)
            dxs = salloc(p0, nc, f"{name}{seq}_dx", [128, NC * T], BF16)
            xxs = salloc(p0, nc, f"{name}{seq}_xx", [128, NC * T], BF16)
            cacc = salloc(p0, nc, f"{name}{seq}_cacc", [128, T], F32)
            w1s = salloc(p0, nc, f"{name}{seq}_w1s", [128, NC * 160], BF16)
            w2s = salloc(p0, nc, f"{name}{seq}_w2s", [32, 5 * 1024], BF16)
            t5f = [salloc(p0, nc, f"{name}{seq}_t5{f}", [32, T], BF16) for f in range(5)]
            mtile = salloc(p0, nc, f"{name}{seq}_mt", [128, 512], F32)
            otile = salloc(p0, nc, f"{name}{seq}_oti", [128, 512], BF16)
            ps5 = palloc(p0, nc, f"{name}{seq}_ps5", [32, 5 * 512], F32)
            psM = palloc(p0, nc, f"{name}{seq}_psM", [128, 2 * 512], F32)

            dv = ctx.dma("sp", xns.rearrange("p (c t) -> p c t", c=NC),
                         xn_r[:, :, s0:s0 + T])
            ctx.dma("pool", w1s[:, 0:NC * 160].rearrange("p (k c) -> p k c", k=NC),
                    W["rw_w1cat"].rearrange("(k p) c -> p k c", p=128))
            dvw = ctx.dma("pool", w2s.rearrange("p (f c) -> p f c", f=5),
                          W["rw_w2cat"].rearrange("(f p) c -> p f c", p=32))
            ctx.wait("dve", "dma_s", dv)
            st_dx = st_xx = 0
            for dc in range(NC):
                x0 = dc * T
                ctx.op("dve", lambda v, dc=dc, x0=x0: v.tensor_scalar_mul(
                    cacc[:, :], xns[:, x0:x0 + T], cw[:, 3 * dc + 1:3 * dc + 2]))
                ctx.op("dve", lambda v, dc=dc, x0=x0: v.scalar_tensor_tensor(
                    cacc[:, 1:T], xns[:, x0:x0 + T - 1], cw[:, 3 * dc:3 * dc + 1],
                    cacc[:, 1:T], ALU.mult, ALU.add))
                ctx.op("dve", lambda v, dc=dc, x0=x0: v.scalar_tensor_tensor(
                    cacc[:, 0:T - 1], xns[:, x0 + 1:x0 + T], cw[:, 3 * dc + 2:3 * dc + 3],
                    cacc[:, 0:T - 1], ALU.mult, ALU.add))
                ctx.op("dve", lambda v, x0=x0: v.tensor_sub(
                    dxs[:, x0:x0 + T], cacc[:, :], xns[:, x0:x0 + T]))
                ctx.op("dve", lambda v, dc=dc, x0=x0: v.scalar_tensor_tensor(
                    xxs[:, x0:x0 + T], dxs[:, x0:x0 + T], maa[:, 6 * dc:6 * dc + 1],
                    xns[:, x0:x0 + T], ALU.mult, ALU.add))
            c_p0 = ctx.c("dve")
            ctx.wait("sp", "dve", c_p0)
            ctx.dma("sp", dx_r[:, :, s0:s0 + T], dxs.rearrange("p (c t) -> p c t", c=NC))
            ctx.wait("pe", "dve", c_p0)
            ctx.wait("pe", "dma_p", dvw)
            ev = ctx.c("act")
            for tc in range(4):
                ctx.wait("pe", "act", ev)
                fns = []
                for f in range(5):
                    for k in range(NC):
                        fns.append(lambda t, f=f, k=k, tc=tc: t.matmul(
                            ps5[:, f * 512:(f + 1) * 512],
                            w1s[:, (k * 160 + f * 32):(k * 160 + (f + 1) * 32)],
                            xxs[:, k * T + tc * 512:k * T + (tc + 1) * 512],
                            start=(k == 0), stop=(k == NC - 1)))
                c_mm = ctx.ops_last_inc("pe", fns)
                ctx.wait("act", "pe", c_mm)
                for f in range(5):
                    ev = ctx.op("act", lambda s, f=f, tc=tc: s.activation(
                        t5f[f][:, tc * 512:(tc + 1) * 512],
                        ps5[:, f * 512:(f + 1) * 512], AF.Tanh))
            ev = ctx.c("dve")
            for f in range(5):
                for oc in range(NC):
                    for tc in range(4):
                        reg = tc % 2
                        ctx.wait("pe", "dve", ev)
                        c_mm = ctx.op("pe", lambda t, f=f, oc=oc, tc=tc, reg=reg: t.matmul(
                            psM[:, reg * 512:(reg + 1) * 512],
                            w2s[:, f * 1024 + oc * 128:f * 1024 + (oc + 1) * 128],
                            t5f[f][:, tc * 512:(tc + 1) * 512], start=True, stop=True))
                        ctx.wait("dve", "pe", c_mm)
                        ctx.op("dve", lambda v, f=f, oc=oc, reg=reg: v.tensor_scalar_add(
                            mtile[:, :], psM[:, reg * 512:(reg + 1) * 512],
                            maa[:, 6 * oc + 1 + f:6 * oc + 2 + f]))
                        ctx.op("dve", lambda v, oc=oc, tc=tc: v.tensor_mul(
                            mtile[:, :], mtile[:, :],
                            dxs[:, oc * T + tc * 512:oc * T + (tc + 1) * 512]))
                        ctx.wait("dve", "dma_s", st_xx)
                        ev = ctx.op("dve", lambda v, oc=oc, tc=tc: v.tensor_add(
                            otile[:, :], mtile[:, :],
                            xns[:, oc * T + tc * 512:oc * T + (tc + 1) * 512]))
                        ctx.wait("sp", "dve", ev)
                        st_xx = ctx.dma("sp", mix_r[f][:, oc, s0 + tc * 512:s0 + (tc + 1) * 512],
                                        otile[:, :])
            ctx.barrier_all()
            p0.close()

            p1 = contextlib.ExitStack()
            wres = salloc(p1, nc, f"{name}{seq}_wres", [128, NC * 1024], BF16)
            xin = salloc(p1, nc, f"{name}{seq}_xin", [128, NC * 512], BF16)
            ptile = salloc(p1, nc, f"{name}{seq}_pt", [64, 16 * 512], BF16)
            th = salloc(p1, nc, f"{name}{seq}_th", [65, T], BF16)
            ew = salloc(p1, nc, f"{name}{seq}_ew", [64, 1024], F32)
            w2aug = salloc(p1, nc, f"{name}{seq}_w2aug", [65, 1024], BF16)
            ps1 = palloc(p1, nc, f"{name}{seq}_ps1", [64, 2 * 512], F32)
            ps2 = palloc(p1, nc, f"{name}{seq}_ps2", [64, 1024], F32)

            def fm64_proj(wap, src_r, dst_r, act=None, biascol=None, store_bf=True):
                ctx.wait("pool", "pe", ctx.c("pe"))
                dvw = ctx.dma("pool", wres.rearrange("p (k c) -> p k c", k=NC),
                              wap.rearrange("(k p) c -> p k c", p=128))
                ev = ctx.c("act")
                evd = ctx.c("dve")
                st_u = 0
                for tc in range(4):
                    ctx.wait("sp", "pe", ctx.c("pe"))
                    dvx = ctx.dma("sp", xin.rearrange("p (c t) -> p c t", c=NC),
                                  src_r[:, :, s0 + tc * 512:s0 + (tc + 1) * 512])
                    ctx.wait("pe", "dma_s", dvx)
                    if tc == 0:
                        ctx.wait("pe", "dma_p", dvw)
                    for h in range(16):
                        reg = h % 2
                        ctx.wait("pe", "act", ev)
                        ctx.wait("pe", "dve", evd)
                        fns = []
                        for k in range(NC):
                            fns.append(lambda t, h=h, k=k, reg=reg: t.matmul(
                                ps1[:, reg * 512:(reg + 1) * 512],
                                wres[:, k * 1024 + h * 64:k * 1024 + (h + 1) * 64],
                                xin[:, k * 512:(k + 1) * 512],
                                start=(k == 0), stop=(k == NC - 1)))
                        c_mm = ctx.ops_last_inc("pe", fns)
                        if act is not None:
                            ctx.wait("act", "pe", c_mm)
                            kw = {}
                            if biascol is not None:
                                kw["bias"] = biascol[:, h:h + 1]
                            ev = ctx.op("act", lambda s, h=h, reg=reg, kw=kw: s.activation(
                                ptile[:, h * 512:(h + 1) * 512],
                                ps1[:, reg * 512:(reg + 1) * 512], act, **kw))
                        else:
                            ctx.wait("dve", "pe", c_mm)
                            evd = ctx.op("dve", lambda v, h=h, reg=reg: v.tensor_copy(
                                ptile[:, h * 512:(h + 1) * 512],
                                ps1[:, reg * 512:(reg + 1) * 512]))
                    c_all = ctx.c("act" if act is not None else "dve")
                    ctx.wait("sp", "act" if act is not None else "dve", c_all)
                    ctx.wait("sp", "dma_s", st_u)
                    st_u = ctx.dma("sp", dst_r[:, :, s0 + tc * 512:s0 + (tc + 1) * 512],
                                   ptile.rearrange("p (h t) -> p h t", h=16))

            fm64_proj(W["rw_Wr"], mix_r[3], r_r)
            fm64_proj(W["rw_Wk"], mix_r[1], k_r)
            fm64_proj(W["rw_Wg"], mix_r[4], g_r, act=AF.Silu)
            fm64_proj(W["rw_Wgate"], dx_r, gate_r, act=AF.Sigmoid, biascol=gateb)

            def tm_proj(wap, src_r, dst_tm):
                ctx.wait("pool", "pe", ctx.c("pe"))
                dvw = ctx.dma("pool", wres.rearrange("p (k c) -> p k c", k=NC),
                              wap.rearrange("(k p) c -> p k c", p=128))
                evd = ctx.c("dve")
                st_u = 0
                for tc in range(4):
                    ctx.wait("sp", "pe", ctx.c("pe"))
                    dvx = ctx.dma("sp", xin.rearrange("p (c t) -> p c t", c=NC),
                                  src_r[:, :, s0 + tc * 512:s0 + (tc + 1) * 512])
                    ctx.wait("pe", "dma_s", dvx)
                    if tc == 0:
                        ctx.wait("pe", "dma_p", dvw)
                    for tt in range(8):
                        ctx.wait("pe", "dve", evd)
                        fns = []
                        for k in range(NC):
                            for j in range(2):
                                fns.append(lambda t, k=k, tt=tt, j=j: t.matmul(
                                    ps2[:, j * 512:(j + 1) * 512],
                                    xin[:, k * 512 + tt * 64:k * 512 + (tt + 1) * 64],
                                    wres[:, k * 1024 + j * 512:k * 1024 + (j + 1) * 512],
                                    start=(k == 0), stop=(k == NC - 1)))
                        c_mm = ctx.ops_last_inc("pe", fns)
                        ctx.wait("dve", "pe", c_mm)
                        ctx.wait("dve", "dma_s", st_u)
                        evd = ctx.op("dve", lambda v: v.tensor_copy(
                            ptile[:, 0:1024], ps2[:, :]))
                        ctx.wait("sp", "dve", evd)
                        st_u = ctx.dma("sp",
                                       dst_tm[s0 + tc * 512 + tt * 64:s0 + tc * 512 + (tt + 1) * 64, :],
                                       ptile[0:64, 0:1024])

            tm_proj(W["rw_Wk"], mix_r[1], S["ktm_d"])
            tm_proj(W["rw_Wv"], mix_r[2], S["vtm_d"])

            ctx.wait("pool", "pe", ctx.c("pe"))
            dvw = ctx.dma("pool", wres[:, 0:NC * 64].rearrange("p (k c) -> p k c", k=NC),
                          W["rw_dw1"].rearrange("(k p) c -> p k c", p=128))
            dvw2 = ctx.dma("pool", w2aug, W["rw_w2aug"])
            c_one = ctx.op("dve", lambda v: v.memset(th[64:65, :], 1.0))
            ev = ctx.c("act")
            for tc in range(4):
                ctx.wait("sp", "pe", ctx.c("pe"))
                dvx = ctx.dma("sp", xin.rearrange("p (c t) -> p c t", c=NC),
                              mix_r[0][:, :, s0 + tc * 512:s0 + (tc + 1) * 512])
                ctx.wait("pe", "dma_s", dvx)
                if tc == 0:
                    ctx.wait("pe", "dma_p", dvw)
                reg = tc % 2
                ctx.wait("pe", "act", ev)
                fns = []
                for k in range(NC):
                    fns.append(lambda t, k=k, reg=reg: t.matmul(
                        ps1[:, reg * 512:(reg + 1) * 512],
                        wres[:, k * 64:(k + 1) * 64],
                        xin[:, k * 512:(k + 1) * 512],
                        start=(k == 0), stop=(k == NC - 1)))
                c_mm = ctx.ops_last_inc("pe", fns)
                ctx.wait("act", "pe", c_mm)
                ev = ctx.op("act", lambda s, tc=tc, reg=reg: s.activation(
                    th[0:64, tc * 512:(tc + 1) * 512],
                    ps1[:, reg * 512:(reg + 1) * 512], AF.Tanh))
            ctx.wait("pe", "act", ev)
            ctx.wait("pe", "dve", c_one)
            ctx.wait("pe", "dma_p", dvw2)
            evd = ctx.c("dve")
            st_u = 0
            for tt in range(T // 64):
                ctx.wait("pe", "dve", evd)
                fns = [lambda t, tt=tt, j=j: t.matmul(
                    ps2[:, j * 512:(j + 1) * 512],
                    th[:, tt * 64:(tt + 1) * 64],
                    w2aug[:, j * 512:(j + 1) * 512], start=True, stop=True)
                    for j in range(2)]
                c_mm = ctx.ops_last_inc("pe", fns)
                ctx.wait("act", "pe", c_mm)
                c_e = ctx.op("act", lambda s: s.activation(ew[:, :], ps2[:, :], AF.Exp))
                ctx.wait("dve", "act", c_e)
                ctx.wait("dve", "dma_s", st_u)
                evd = ctx.op("dve", lambda v: v.tensor_scalar_min(ew[:, :], ew[:, :], 1.2))
                ctx.wait("sp", "dve", evd)
                st_u = ctx.dma("sp", S["expw_d"][s0 + tt * 64:s0 + (tt + 1) * 64, :],
                               ew[0:64, :])
            ctx.barrier_all()
            p1.close()

            p2 = contextlib.ExitStack()
            r64 = salloc(p2, nc, f"{name}{seq}_r64", [64, 16 * C], BF16)
            k64 = salloc(p2, nc, f"{name}{seq}_k64", [64, 16 * C], BF16)
            ktm = salloc(p2, nc, f"{name}{seq}_ktm2", [64, 1024], BF16)
            vtm = salloc(p2, nc, f"{name}{seq}_vtm2", [64, 1024], BF16)
            ewc = salloc(p2, nc, f"{name}{seq}_ewc", [64, 1024], F32)
            a_tm = salloc(p2, nc, f"{name}{seq}_atm", [64, 1024], F32)
            ea_tm = salloc(p2, nc, f"{name}{seq}_eatm", [64, 1024], BF16)
            a64 = salloc(p2, nc, f"{name}{seq}_a64", [64, 1024], F32)
            ea64 = salloc(p2, nc, f"{name}{seq}_ea64", [64, 1024], BF16)
            ena64 = salloc(p2, nc, f"{name}{seq}_ena64", [64, 1024], BF16)
            eaend = salloc(p2, nc, f"{name}{seq}_eaend", [64, 16], F32)
            qg64 = salloc(p2, nc, f"{name}{seq}_qg64", [64, 16 * C], BF16)
            kg64 = salloc(p2, nc, f"{name}{seq}_kg64", [64, 16 * C], BF16)
            kgtm = salloc(p2, nc, f"{name}{seq}_kgtm", [64, 1024], BF16)
            Psb = salloc(p2, nc, f"{name}{seq}_Psb", [64, 16 * C], BF16)
            ssb = salloc(p2, nc, f"{name}{seq}_ssb", [64, 16 * C], F32)
            ssbb = salloc(p2, nc, f"{name}{seq}_ssbb", [64, 16 * C], BF16)
            ysb = salloc(p2, nc, f"{name}{seq}_ysb", [64, 16 * C], F32)
            psWY = palloc(p2, nc, f"{name}{seq}_psWY", [64, 1024], F32)
            psTS = palloc(p2, nc, f"{name}{seq}_psTS", [64, 1024], F32)
            psP = palloc(p2, nc, f"{name}{seq}_psP", [64, 1024], F32)

            for direction in (0, 1):
                ctx.op("dve", lambda v: v.memset(ssb[:, :], 0.0))
                c_state = ctx.op("dve", lambda v: v.memset(ssbb[:, :], 0.0))
                chunks = range(NCH) if direction == 0 else range(NCH - 1, -1, -1)
                for ci in chunks:
                    t0 = s0 + ci * C
                    ctx.wait("sp", "dve", ctx.c("dve"))
                    ctx.wait("sp", "act", ctx.c("act"))
                    ctx.wait("sp", "pe", ctx.c("pe"))
                    ctx.wait("sp", "dma_s", ctx.c("dma_s"))
                    ctx.dma("sp", r64.rearrange("p (h t) -> p h t", h=16),
                            r_r[:, :, t0:t0 + C])
                    ctx.dma("sp", k64.rearrange("p (h t) -> p h t", h=16),
                            k_r[:, :, t0:t0 + C])
                    ctx.dma("sp", ktm[0:64, :], S["ktm_d"][t0:t0 + C, :])
                    ctx.dma("sp", vtm[0:64, :], S["vtm_d"][t0:t0 + C, :])
                    dvl = ctx.dma("sp", ewc[0:64, :], S["expw_d"][t0:t0 + C, :])
                    ctx.wait("pe", "dma_s", dvl)
                    fns = [lambda t, j=j: t.matmul(
                        psWY[:, j * 512:(j + 1) * 512], cst.cumsum_neg[:, :],
                        ewc[:, j * 512:(j + 1) * 512], start=True, stop=True)
                        for j in range(2)]
                    c_mm = ctx.ops_last_inc("pe", fns)
                    ctx.wait("dve", "pe", c_mm)
                    c_a = ctx.op("dve", lambda v: v.tensor_copy(a_tm[:, :], psWY[:, :]))
                    ctx.wait("act", "dve", c_a)
                    c_e1 = ctx.op("act", lambda s, d=direction: s.activation(
                        ea_tm[:, :], a_tm[:, :], AF.Exp, scale=(-1.0 if d == 0 else 1.0)))
                    ctx.wait("pe", "dve", c_a)
                    fns = []
                    for h in range(16):
                        fns.append(lambda t, h=h: t.transpose(
                            psTS[:, h * 64:(h + 1) * 64],
                            a_tm[:, h * 64:(h + 1) * 64], cst.identf[0:64, 0:64]))
                    c_tr = ctx.ops_last_inc("pe", fns)
                    ctx.wait("dve", "pe", c_tr)
                    c_a64 = ctx.op("dve", lambda v: v.tensor_copy(a64[:, :], psTS[:, :]))
                    ctx.wait("act", "dve", c_a64)
                    ctx.op("act", lambda s: s.activation(ea64[:, :], a64[:, :], AF.Exp))
                    ctx.op("act", lambda s: s.activation(ena64[:, :], a64[:, :], AF.Exp, scale=-1.0))
                    c_ee = ctx.op("act", lambda s: s.activation(
                        eaend.rearrange("p (h u) -> p h u", u=1),
                        a64.rearrange("p (h t) -> p h t", t=C)[:, :, C - 1:C], AF.Exp))
                    ctx.wait("dve", "act", c_ee)
                    if direction == 0:
                        ctx.op("dve", lambda v: v.tensor_mul(qg64[:, :], r64[:, :], ea64[:, :]))
                        ctx.op("dve", lambda v: v.tensor_mul(kg64[:, :], k64[:, :], ena64[:, :]))
                    else:
                        ctx.op("dve", lambda v: v.tensor_mul(qg64[:, :], r64[:, :], ena64[:, :]))
                        ctx.op("dve", lambda v: v.tensor_mul(kg64[:, :], k64[:, :], ea64[:, :]))
                    c_kg = ctx.op("dve", lambda v: v.tensor_mul(kgtm[:, :], ktm[:, :], ea_tm[:, :]))
                    ctx.wait("pe", "dve", c_kg)
                    fns = []
                    for h in range(16):
                        fns.append(lambda t, h=h: t.matmul(
                            psP[:, h * 64:(h + 1) * 64],
                            kg64[:, h * C:(h + 1) * C],
                            qg64[:, h * C:(h + 1) * C],
                            start=(h % 8 == 0), stop=True))
                    c_mm = ctx.ops_last_inc("pe", fns)
                    ctx.wait("dve", "pe", c_mm)
                    mask = cst.mask_f if direction == 0 else cst.mask_b
                    c_P = ctx.op("dve", lambda v, mask=mask: v.tensor_tensor(
                        Psb.rearrange("p (h t) -> p h t", h=16), psP.rearrange("p (h t) -> p h t", h=16),
                        mask.rearrange("p (u t) -> p u t", u=1).broadcast_to((64, 16, 64)),
                        ALU.mult))
                    if direction == 1:
                        ctx.op("dve", lambda v: v.tensor_mul(
                            ssb.rearrange("p (h t) -> p h t", h=16),
                            ssb.rearrange("p (h t) -> p h t", h=16),
                            eaend.rearrange("p (h u) -> p h u", u=1).broadcast_to((64, 16, 64))))
                    c_state = ctx.op("dve", lambda v: v.tensor_copy(ssbb[:, :], ssb[:, :]))
                    ctx.wait("pe", "dve", max(c_P, c_state))
                    fns = []
                    for h in range(16):
                        fns.append(lambda t, h=h: t.matmul(
                            psWY[:, h * 64:(h + 1) * 64],
                            vtm[0:64, h * 64:(h + 1) * 64],
                            Psb[:, h * C:(h + 1) * C],
                            start=(h % 8 == 0), stop=False))
                        fns.append(lambda t, h=h: t.matmul(
                            psWY[:, h * 64:(h + 1) * 64],
                            ssbb[:, h * C:(h + 1) * C],
                            qg64[:, h * C:(h + 1) * C],
                            start=False, stop=True))
                    c_my = ctx.ops_last_inc("pe", fns)
                    ctx.wait("dve", "pe", c_my)
                    if direction == 0:
                        c_y = ctx.op("dve", lambda v: v.tensor_copy(ysb[:, :], psWY[:, :]))
                    else:
                        dvy = ctx.dma("sp", ysb.rearrange("p (h t) -> p h t", h=16),
                                      y_r[:, :, t0:t0 + C])
                        ctx.wait("dve", "dma_s", dvy)
                        c_y = ctx.op("dve", lambda v: v.tensor_add(
                            ysb[:, :], ysb[:, :], psWY[:, :]))
                    ctx.wait("sp", "dve", c_y)
                    ctx.dma("sp", y_r[:, :, t0:t0 + C],
                            ysb.rearrange("p (h t) -> p h t", h=16))
                    fns = []
                    for h in range(16):
                        fns.append(lambda t, h=h: t.matmul(
                            psTS[:, h * 64:(h + 1) * 64],
                            kgtm[:, h * 64:(h + 1) * 64],
                            vtm[0:64, h * 64:(h + 1) * 64],
                            start=(h % 8 == 0), stop=True))
                    c_ms = ctx.ops_last_inc("pe", fns)
                    ctx.wait("dve", "pe", c_ms)
                    if direction == 0:
                        ctx.op("dve", lambda v: v.tensor_add(ssb[:, :], ssb[:, :], psTS[:, :]))
                        ctx.op("dve", lambda v: v.tensor_mul(
                            ssb.rearrange("p (h t) -> p h t", h=16),
                            ssb.rearrange("p (h t) -> p h t", h=16),
                            eaend.rearrange("p (h u) -> p h u", u=1).broadcast_to((64, 16, 64))))
                        c_state = ctx.op("dve", lambda v: v.tensor_copy(ssbb[:, :], ssb[:, :]))
                    else:
                        c_state = ctx.op("dve", lambda v: v.tensor_add(
                            ssb[:, :], ssb[:, :], psTS[:, :]))
            ctx.barrier_all()
            p2.close()


        p3 = contextlib.ExitStack()
        y64 = salloc(p3, nc, f"{name}{seq}_y64", [64, 16 * 512], F32)
        gt64 = salloc(p3, nc, f"{name}{seq}_gt", [64, 16 * 512], BF16)
        gg64 = salloc(p3, nc, f"{name}{seq}_gg", [64, 16 * 512], BF16)
        ygb = salloc(p3, nc, f"{name}{seq}_ygb", [64, 16 * 512], BF16)
        ygq = salloc(p3, nc, f"{name}{seq}_ygq", [64, 16 * 512], BF16)
        gmean = salloc(p3, nc, f"{name}{seq}_gm", [64, 512], F32)
        gmsq = salloc(p3, nc, f"{name}{seq}_gms", [64, 512], F32)
        grstd = salloc(p3, nc, f"{name}{seq}_gr", [64, 512], F32)
        yo = salloc(p3, nc, f"{name}{seq}_yo", [64, 16 * 512], BF16)
        prev_mm = 0
        wo = salloc(p3, nc, f"{name}{seq}_wo", [64, 16 * 128], BF16)
        rt3 = salloc(p3, nc, f"{name}{seq}_rt3", [128, 512], F32)
        psG = palloc(p3, nc, f"{name}{seq}_psG", [64, 1024], F32)
        psO = palloc(p3, nc, f"{name}{seq}_psO", [128, 2 * 512], F32)
        st_u = 0
        for tc in range(4):
            ctx.wait("sp", "dve", ctx.c("dve"))
            ctx.wait("sp", "act", ctx.c("act"))
            ctx.wait("dve", "pe", ctx.c("pe"))
            ctx.dma("sp", y64.rearrange("p (h t) -> p h t", h=16),
                    y_r[:, :, s0 + tc * 512:s0 + (tc + 1) * 512])
            ctx.dma("sp", gt64.rearrange("p (h t) -> p h t", h=16),
                    gate_r[:, :, s0 + tc * 512:s0 + (tc + 1) * 512])
            dvl = ctx.dma("sp", gg64.rearrange("p (h t) -> p h t", h=16),
                          g_r[:, :, s0 + tc * 512:s0 + (tc + 1) * 512])
            ctx.wait("dve", "dma_s", dvl)
            ctx.op("dve", lambda v: v.tensor_mul(y64[:, :], y64[:, :], gt64[:, :]))
            c_yg = ctx.op("dve", lambda v: v.tensor_copy(ygb[:, :], y64[:, :]))
            ctx.wait("act", "dve", c_yg)
            c_sq = ctx.op("act", lambda s: s.activation(ygq[:, :], y64[:, :], AF.Square))
            ctx.wait("pe", "act", c_sq)
            ctx.wait("pe", "dve", c_yg)
            for h in range(16):
                c_mm = None
                fns = [lambda t, h=h: t.matmul(
                    psG[:, 0:512], cst.ones[0:64, 0:64],
                    ygb[:, h * 512:(h + 1) * 512], start=True, stop=True),
                    lambda t, h=h: t.matmul(
                    psG[:, 512:1024], cst.ones[0:64, 0:64],
                    ygq[:, h * 512:(h + 1) * 512], start=True, stop=True)]
                c_mm = ctx.ops_last_inc("pe", fns)
                ctx.wait("act", "pe", c_mm)
                ctx.op("act", lambda s: s.activation(
                    gmean[:, :], psG[:, 0:512], AF.Copy, scale=1.0 / 64))
                c_m2 = ctx.op("act", lambda s: s.activation(gmsq[:, :], gmean[:, :], AF.Square))
                ctx.wait("dve", "act", c_m2)
                c_var = ctx.op("dve", lambda v: v.scalar_tensor_tensor(
                    grstd[:, :], psG[:, 512:1024], 1.0 / 64, gmsq[:, :],
                    ALU.mult, ALU.subtract))
                ctx.wait("act", "dve", c_var)
                ctx.op("act", lambda s: s.activation(
                    grstd[:, :], grstd[:, :], AF.Ln,
                    bias=cst.fscal[0:64, FS[6.4e-4]:FS[6.4e-4] + 1]))
                c_rs = ctx.op("act", lambda s: s.activation(
                    grstd[:, :], grstd[:, :], AF.Exp, scale=-0.5))
                ctx.wait("dve", "act", c_rs)
                ctx.op("dve", lambda v, h=h: v.tensor_sub(
                    y64[:, h * 512:(h + 1) * 512], y64[:, h * 512:(h + 1) * 512],
                    gmean[:, :]))
                ctx.op("dve", lambda v, h=h: v.tensor_mul(
                    y64[:, h * 512:(h + 1) * 512], y64[:, h * 512:(h + 1) * 512],
                    grstd[:, :]))
                ctx.op("dve", lambda v, h=h: v.tensor_scalar(
                    y64[:, h * 512:(h + 1) * 512], y64[:, h * 512:(h + 1) * 512],
                    lnw[:, h:h + 1], lnb[:, h:h + 1], ALU.mult, ALU.add))
                c_n = ctx.op("dve", lambda v, h=h: v.tensor_mul(
                    yo[:, h * 512:(h + 1) * 512], y64[:, h * 512:(h + 1) * 512],
                    gg64[:, h * 512:(h + 1) * 512]))
                ctx.wait("pe", "dve", c_n)
                ctx.wait("pe", "act", c_rs)
            c_yo = ctx.c("dve")
            for oc in range(NC):
                reg = oc % 2
                ctx.wait("pool", "pe", prev_mm)
                dvw = ctx.dma("pool", wo.rearrange("p (h c) -> p h c", h=16),
                              W["rw_Wo"][:, oc * 128:(oc + 1) * 128].rearrange(
                                  "(h p) c -> p h c", p=64))
                ctx.wait("sp", "dve", ctx.c("dve"))
                ctx.wait("sp", "dma_s", st_u)
                dvr = ctx.dma("sp", rt3[:, :],
                              res_in_r[:, oc, s0 + tc * 512:s0 + (tc + 1) * 512])
                ctx.wait("pe", "dma_p", dvw)
                ctx.wait("pe", "dve", c_yo)
                fns = []
                for h in range(16):
                    fns.append(lambda t, h=h, reg=reg: t.matmul(
                        psO[:, reg * 512:(reg + 1) * 512],
                        wo[:, h * 128:(h + 1) * 128],
                        yo[:, h * 512:(h + 1) * 512],
                        start=(h == 0), stop=(h == 15)))
                c_mm = ctx.ops_last_inc("pe", fns)
                prev_mm = c_mm
                ctx.wait("dve", "pe", c_mm)
                ctx.wait("dve", "dma_s", dvr)
                c_o = ctx.op("dve", lambda v, reg=reg: v.tensor_add(
                    rt3[:, :], rt3[:, :], psO[:, reg * 512:(reg + 1) * 512]))
                ctx.wait("sp", "dve", c_o)
                st_u = ctx.dma("sp", res_out_r[:, oc, s0 + tc * 512:s0 + (tc + 1) * 512],
                               rt3[:, :])
                c_yo = ctx.c("dve")
        ctx.barrier_all()
        p3.close()
    ostack.close()


def emit_moe(ctx, cst, xn_fm, xn_tm, xn32, res_in, res_out, W, S, initidx, name="moe"):
    nc = ctx.nc
    CAP = 1024
    NOC = 22
    bc_slot = nc.gpsimd.alloc_register(f"{name}_bcs")
    bc_tok = nc.gpsimd.alloc_register(f"{name}_bct")
    ctx.op("pool", lambda g: g.reg_mov(bc_slot, 8 * CAP - 1), inc=False)
    ctx.op("pool", lambda g: g.reg_mov(bc_tok, NT - 1), inc=False)
    xn_r = xn_fm.rearrange("(c p) t -> p c t", p=128)
    xn32_r = xn32.rearrange("(c p) t -> p c t", p=128)
    res_in_r = res_in.rearrange("(c p) t -> p c t", p=128)
    res_out_r = res_out.rearrange("(c p) t -> p c t", p=128)

    ctx.barrier_all()
    pa = contextlib.ExitStack()
    rout = salloc(pa, nc, f"{name}_rout", [128, NC * 8], F32)
    xt = salloc(pa, nc, f"{name}_xt", [128, NC * 128], F32)
    lg = salloc(pa, nc, f"{name}_lg", [128, 8], F32)
    self32 = salloc(pa, nc, f"{name}_self32", [128, 8], F32)
    mxc = salloc(pa, nc, f"{name}_mxc", [128, 8], F32)
    base_f = salloc(pa, nc, f"{name}_basef", [1, 8], F32)
    slotf = salloc(pa, nc, f"{name}_slotf", [128, 4], F32)
    slotI = salloc(pa, nc, f"{name}_slotI", [128, 1], I32)
    tokI = salloc(pa, nc, f"{name}_tokI", [128, 1], I32)
    psL = palloc(pa, nc, f"{name}_psL", [128, 8], F32)
    psP = palloc(pa, nc, f"{name}_psP", [128, 8], F32)
    psB = palloc(pa, nc, f"{name}_psB", [1, 8], F32)

    ctx.dma("pool", S["tokidx_d"], initidx)
    dvr = ctx.dma("pool", rout.rearrange("p (k c) -> p k c", k=NC),
                  W["moe_router"].rearrange("(k p) c -> p k c", p=128))
    c_b0 = ctx.op("dve", lambda v: v.memset(base_f[:, :], 0.0))
    ctx.wait("pe", "dma_p", dvr)
    sc_done = 0
    for tt in range(NT // 128):
        t0 = tt * 128
        ctx.wait("sp", "pe", ctx.c("pe"))
        ctx.wait("dve", "dma_p", sc_done)
        dvx = ctx.dma("sp", xt.rearrange("p (c t) -> p c t", c=NC),
                      xn32_r[:, :, t0:t0 + 128])
        ctx.wait("pe", "dma_s", dvx)
        ctx.wait("pe", "dve", ctx.c("dve"))
        fns = [lambda t, k=k: t.matmul(psL[:, :], xt[:, k * 128:(k + 1) * 128],
                                       rout[:, k * 8:(k + 1) * 8],
                                       start=(k == 0), stop=(k == NC - 1))
               for k in range(NC)]
        c_mm = ctx.ops_last_inc("pe", fns)
        ctx.wait("dve", "pe", c_mm)
        ctx.op("dve", lambda v: v.tensor_copy(lg[:, :], psL[:, :]))
        ctx.op("dve", lambda v: v.reduce_max(mxc[:, 0:1], lg[:, :], axis=AX.X))
        ctx.op("dve", lambda v: v.memset(mxc[:, 1:2], 1.0))
        ctx.op("dve", lambda v: v.memset(slotf[:, 0:1], 0.0))
        for e in range(8):
            ctx.op("dve", lambda v, e=e: v.tensor_scalar(
                mxc[:, 2:3], lg[:, e:e + 1], mxc[:, 0:1], None, ALU.is_ge))
            ctx.op("dve", lambda v, e=e: v.tensor_mul(
                self32[:, e:e + 1], mxc[:, 2:3], mxc[:, 1:2]))
            ctx.op("dve", lambda v, e=e: v.tensor_sub(
                mxc[:, 1:2], mxc[:, 1:2], self32[:, e:e + 1]))
            ctx.op("dve", lambda v, e=e: v.scalar_tensor_tensor(
                slotf[:, 0:1], self32[:, e:e + 1], float(e), slotf[:, 0:1],
                ALU.mult, ALU.add))
        c_sel = ctx.c("dve")
        ctx.wait("pe", "dve", c_sel)
        fns = [
            lambda t: t.matmul(psP[:, :], cst.prefix32[:, :], self32[:, :],
                               start=True, stop=False),
            lambda t: t.matmul(psP[:, :], cst.ones32[0:1, :], base_f[:, :],
                               start=False, stop=True),
        ]
        c_pm = ctx.ops_last_inc("pe", fns)
        c_bm = ctx.op("pe", lambda t: t.matmul(psB[:, :], cst.ones32[:, 0:1],
                                               self32[:, :], start=True, stop=True))
        ctx.wait("dve", "pe", c_bm)
        ctx.op("dve", lambda v: v.tensor_mul(self32[:, :], self32[:, :], psP[:, :]))
        ctx.op("dve", lambda v: v.tensor_reduce(
            slotf[:, 1:2].rearrange("p (a u) -> p a u", u=1),
            self32.rearrange("p (a e) -> p a e", a=1), AX.X, ALU.add))
        ctx.op("dve", lambda v: v.tensor_single_scalar(
            slotf[:, 2:3], slotf[:, 1:2], float(CAP), ALU.is_lt))
        ctx.op("dve", lambda v: v.scalar_tensor_tensor(
            slotf[:, 3:4], slotf[:, 0:1], float(CAP), slotf[:, 1:2],
            ALU.mult, ALU.add))
        ctx.op("dve", lambda v: v.tensor_mul(slotf[:, 3:4], slotf[:, 3:4], slotf[:, 2:3]))
        ctx.op("dve", lambda v: v.tensor_scalar(
            slotf[:, 2:3], slotf[:, 2:3], -1.0, -float(1 << 30), ALU.add, ALU.mult))
        ctx.op("dve", lambda v: v.tensor_add(slotf[:, 3:4], slotf[:, 3:4], slotf[:, 2:3]))
        c_si = ctx.op("dve", lambda v: v.tensor_copy(slotI[:, :], slotf[:, 3:4]))
        ctx.op("dve", lambda v, t0=t0: v.tensor_scalar_add(
            slotf[:, 0:1], cst.iota128[:, :], float(t0)))
        c_ti = ctx.op("dve", lambda v: v.tensor_copy(tokI[:, :], slotf[:, 0:1]))
        ctx.wait("pool", "dve", c_ti)
        sc_done = ctx.idma(S["tokidx_d"].rearrange("(s u) -> s u", u=1),
                 IndirectOffsetOnAxis(slotI[:, :], 0),
                 tokI[:, :], None, bounds_check=bc_slot, oob_is_err=False)
        c_bu = ctx.op("dve", lambda v: v.tensor_add(base_f[:, :], base_f[:, :], psB[:, :]))
        ctx.wait("pe", "dve", c_bu)
    ctx.barrier_all()
    pa.close()

    pb = contextlib.ExitStack()
    idxs = salloc(pb, nc, f"{name}_idxs", [128, 8], I32)
    xgt = salloc(pb, nc, f"{name}_xgt", [128, 1024], BF16)
    xef = salloc(pb, nc, f"{name}_xef", [128, NC * CAP], BF16)
    gpan = [salloc(pb, nc, f"{name}_gp{i}", [128, NC * 128], BF16) for i in range(2)]
    upan = [salloc(pb, nc, f"{name}_up{i}", [128, NC * 128], BF16) for i in range(2)]
    dpan = [salloc(pb, nc, f"{name}_dp{i}", [128, NOC * 128], BF16) for i in range(2)]
    atmp = salloc(pb, nc, f"{name}_at", [128, 1024], F32)
    hbuf = salloc(pb, nc, f"{name}_hb", [128, NOC * CAP], BF16)
    yfm = salloc(pb, nc, f"{name}_yfm", [128, NC * CAP], BF16)
    ytile = salloc(pb, nc, f"{name}_ytl", [128, 1024], BF16)
    psG = palloc(pb, nc, f"{name}_psG", [128, 2 * 512], F32)
    psU = palloc(pb, nc, f"{name}_psU", [128, 2 * 512], F32)
    psY = palloc(pb, nc, f"{name}_psY", [128, 2 * 512], F32)
    psT = palloc(pb, nc, f"{name}_psT", [128, 1024], BF16)

    pan_use = {("g", 0): 0, ("g", 1): 0, ("u", 0): 0, ("u", 1): 0,
               ("d", 0): 0, ("d", 1): 0}
    for e in range(8):
        ctx.wait("pool", "pe", ctx.c("pe"))
        ctx.wait("pool", "dma_p", ctx.c("dma_p"))
        for st in range(CAP // 128):
            dvi = ctx.dma("pool", idxs[:, st:st + 1],
                          S["tokidx_d"].rearrange("(s u) -> s u", u=1)[
                              e * CAP + st * 128:e * CAP + (st + 1) * 128, :])
            ctx.wait("pool", "dma_p", dvi)
            ctx.wait("pool", "dve", ctx.c("dve"))
            dvg = ctx.idma(xgt[:, :], None, xn_tm[:, :],
                           IndirectOffsetOnAxis(idxs[:, st:st + 1], 0),
                           bounds_check=bc_tok, oob_is_err=False)
            ctx.wait("pe", "dma_p", dvg)
            fns = [lambda t, c=c: t.transpose(
                psT[:, c * 128:(c + 1) * 128], xgt[:, c * 128:(c + 1) * 128],
                cst.ident[:, :]) for c in range(NC)]
            c_tr = ctx.ops_last_inc("pe", fns)
            ctx.wait("dve", "pe", c_tr)
            c_ev = ctx.op("dve", lambda v, st=st: v.tensor_copy(
                xef.rearrange("p (c s) -> p c s", c=NC)[:, :, st * 128:(st + 1) * 128],
                psT.rearrange("p (c s) -> p c s", c=NC)))
            ctx.wait("pool", "dve", c_ev)
            ctx.wait("pe", "dve", c_ev)
        ev_a = ctx.c("act")
        ev_d = ctx.c("dve")
        for oc in range(NOC):
            par = oc % 2
            ctx.wait("pool", "pe", pan_use[("g", par)])
            dv1 = ctx.dma("pool", gpan[par].rearrange("p (k c) -> p k c", k=NC),
                          W["moe_gateT"][e * 1024:(e + 1) * 1024,
                                         oc * 128:(oc + 1) * 128].rearrange(
                              "(k p) c -> p k c", p=128))
            dv2 = ctx.dma("pool", upan[par].rearrange("p (k c) -> p k c", k=NC),
                          W["moe_upT"][e * 1024:(e + 1) * 1024,
                                       oc * 128:(oc + 1) * 128].rearrange(
                              "(k p) c -> p k c", p=128))
            ctx.wait("pe", "dma_p", dv2)
            ctx.wait("pe", "act", ev_a)
            ctx.wait("pe", "dve", ev_d)
            fns = []
            for k in range(NC):
                for blk in range(2):
                    fns.append(lambda t, k=k, blk=blk, par=par: t.matmul(
                        psG[:, blk * 512:(blk + 1) * 512],
                        gpan[par][:, k * 128:(k + 1) * 128],
                        xef[:, k * CAP + blk * 512:k * CAP + (blk + 1) * 512],
                        start=(k == 0), stop=(k == NC - 1)))
                    fns.append(lambda t, k=k, blk=blk, par=par: t.matmul(
                        psU[:, blk * 512:(blk + 1) * 512],
                        upan[par][:, k * 128:(k + 1) * 128],
                        xef[:, k * CAP + blk * 512:k * CAP + (blk + 1) * 512],
                        start=(k == 0), stop=(k == NC - 1)))
            c_mm = ctx.ops_last_inc("pe", fns)
            pan_use[("g", par)] = c_mm
            ctx.wait("act", "pe", c_mm)
            ctx.wait("act", "dve", ctx.c("dve"))
            ev_a = ctx.op("act", lambda s: s.activation(atmp[:, :], psG[:, :], AF.Silu))
            ctx.wait("dve", "act", ev_a)
            ev_d = ctx.op("dve", lambda v, oc=oc: v.tensor_mul(
                hbuf[:, oc * CAP:(oc + 1) * CAP], psU[:, :], atmp[:, :]))
        ev_d2 = ctx.c("dve")
        for ocd in range(NC):
            par = ocd % 2
            ctx.wait("pool", "pe", pan_use[("d", par)])
            dvd = ctx.dma("pool", dpan[par].rearrange("p (k c) -> p k c", k=NOC),
                          W["moe_downT"][e * 2816:(e + 1) * 2816,
                                         ocd * 128:(ocd + 1) * 128].rearrange(
                              "(k p) c -> p k c", p=128))
            ctx.wait("pe", "dma_p", dvd)
            ctx.wait("pe", "dve", ev_d2)
            fns = []
            for k in range(NOC):
                for blk in range(2):
                    fns.append(lambda t, k=k, blk=blk, par=par: t.matmul(
                        psY[:, blk * 512:(blk + 1) * 512],
                        dpan[par][:, k * 128:(k + 1) * 128],
                        hbuf[:, k * CAP + blk * 512:k * CAP + (blk + 1) * 512],
                        start=(k == 0), stop=(k == NOC - 1)))
            c_mm = ctx.ops_last_inc("pe", fns)
            pan_use[("d", par)] = c_mm
            ctx.wait("dve", "pe", c_mm)
            ev_d2 = ctx.op("dve", lambda v, ocd=ocd: v.tensor_copy(
                yfm[:, ocd * CAP:(ocd + 1) * CAP], psY[:, :]))
        c_prev = ctx.c("dve")
        for st in range(CAP // 128):
            ctx.wait("pe", "dve", c_prev)
            fns = [lambda t, c=c, st=st: t.transpose(
                psT[:, c * 128:(c + 1) * 128],
                yfm[:, c * CAP + st * 128:c * CAP + (st + 1) * 128],
                cst.ident[:, :]) for c in range(NC)]
            c_tr = ctx.ops_last_inc("pe", fns)
            ctx.wait("dve", "pe", c_tr)
            ctx.wait("dve", "dma_p", ctx.c("dma_p"))
            c_prev = ctx.op("dve", lambda v: v.tensor_copy(ytile[:, :], psT[:, :]))
            ctx.wait("pool", "dve", c_prev)
            ctx.idma(S["ytm_d"][:, :],
                     IndirectOffsetOnAxis(idxs[:, st:st + 1], 0),
                     ytile[:, :], None, bounds_check=bc_tok, oob_is_err=False)
    ctx.barrier_all()
    pb.close()

    pc = contextlib.ExitStack()
    ytl = salloc(pc, nc, f"{name}_ytl2", [128, 1024], BF16)
    yft = salloc(pc, nc, f"{name}_yft", [128, NC * 128], BF16)
    rtl = salloc(pc, nc, f"{name}_rtl", [128, NC * 128], F32)
    psT2 = palloc(pc, nc, f"{name}_psT2", [128, 1024], BF16)
    st_u = 0
    for tt in range(NT // 128):
        t0 = tt * 128
        ctx.wait("sp", "pe", ctx.c("pe"))
        ctx.wait("sp", "dve", ctx.c("dve"))
        ctx.wait("sp", "dma_s", st_u)
        dvy = ctx.dma("sp", ytl[:, :], S["ytm_d"][t0:t0 + 128, :])
        dvr = ctx.dma("sp", rtl.rearrange("p (c t) -> p c t", c=NC),
                      res_in_r[:, :, t0:t0 + 128])
        ctx.wait("pe", "dma_s", dvr)
        fns = [lambda t, c=c: t.transpose(
            psT2[:, c * 128:(c + 1) * 128], ytl[:, c * 128:(c + 1) * 128],
            cst.ident[:, :]) for c in range(NC)]
        c_tr = ctx.ops_last_inc("pe", fns)
        ctx.wait("dve", "pe", c_tr)
        ctx.op("dve", lambda v: v.tensor_copy(yft[:, :], psT2[:, :]))
        c_o = ctx.op("dve", lambda v: v.tensor_add(rtl[:, :], rtl[:, :], yft[:, :]))
        ctx.wait("sp", "dve", c_o)
        st_u = ctx.dma("sp", res_out_r[:, :, t0:t0 + 128],
                       rtl.rearrange("p (c t) -> p c t", c=NC))
        ctx.wait("pe", "dve", c_o)
    ctx.barrier_all()
    pc.close()


def build_full(dbg_stream=False):
    nc = bass.Bass("TRN2", target_bir_lowering=False)
    ctx = Ctx(nc)
    DT = nc.dram_tensor
    resA = DT("x_fm", [D, NT], F32, kind="ExternalInput").ap()
    resB = DT("resB", [D, NT], F32).ap()
    resC = DT("resC", [D, NT], F32).ap()
    out = DT("out_fm", [D, NT], F32, kind="ExternalOutput").ap()

    W = {}
    specs = [
        ("mla_WDQ", [1024, 32]), ("mla_WDKV", [1024, 32]), ("mla_WKR", [1024, 32]),
        ("mla_WUQ", [32, 1024]), ("mla_WUK", [32, 1024]), ("mla_WUV", [32, 1024]),
        ("mla_WQR", [32, 256]), ("mla_WO", [1024, 1024]), ("mla_bUV", [1024]),
        ("mla_bsmall", [32, 3]), ("mla_bUQ2", [128, 8]), ("mla_bUK2", [128, 8]),
        ("mla_bQR2", [32, 8]), ("mla_bO2", [128, 8]),
        ("ffn_w1T", [1024, 4096]), ("ffn_w2T", [1024, 4096]),
        ("ffn_h1T", [4096, 4096]), ("ffn_h2T", [4096, 4096]), ("ffn_w3T", [4096, 1024]),
        ("ffn_s1", [128, 32]), ("ffn_s2", [128, 32]), ("ffn_hs1", [128, 32]),
        ("ffn_hs2", [128, 32]), ("ffn_s3", [128, 8]),
        ("rw_Wr", [1024, 1024]), ("rw_Wk", [1024, 1024]), ("rw_Wv", [1024, 1024]),
        ("rw_Wg", [1024, 1024]), ("rw_Wgate", [1024, 1024]), ("rw_Wo", [1024, 1024]),
        ("rw_w1cat", [1024, 160]), ("rw_w2cat", [160, 1024]), ("rw_dw1", [1024, 64]),
        ("rw_w2aug", [65, 1024]), ("rw_maa", [128, 48]), ("rw_convw", [128, 24]),
        ("rw_gateb64", [64, 16]), ("rw_lnw64", [64, 16]), ("rw_lnb64", [64, 16]),
        ("mam_Win", [1024, 4096]), ("mam_Wxp", [2048, 96]), ("mam_Wdt", [64, 2048]),
        ("mam_Wout", [2048, 1024]), ("mam_convw", [128, 64]), ("mam_convb", [128, 16]),
        ("mam_bdt", [128, 16]), ("mam_Alog", [128, 256]), ("mam_D", [128, 16]),
        ("mam_lnw2", [128, 8]), ("mam_lnb2", [128, 8]),
        ("moe_router", [1024, 8]), ("moe_gateT", [8 * 1024, 2816]),
        ("moe_upT", [8 * 1024, 2816]), ("moe_downT", [8 * 2816, 1024]),
        ("lat_WqT", [1024, 1024]), ("lat_WkT", [1024, 1024]), ("lat_WvT", [1024, 1024]),
        ("lat_WdT", [1024, 1024]), ("lat_bq2", [128, 8]), ("lat_bd2", [128, 8]),
        ("lat_bkv", [2048]), ("lat_gam2", [128, 8]), ("lat_bet2", [128, 8]),
    ]
    for nm, shape in specs:
        W[nm] = DT(nm, shape, F32, kind="ExternalInput").ap()
    initidx = DT("c_initidx", [8 * 1024], I32, kind="ExternalInput").ap()

    xn_fm = DT("xn_fm", [D, NT], BF16).ap()
    xn2_fm = DT("xn2_fm", [D, NT], BF16).ap()
    xn_tm = DT("xn_tm", [NT, D], BF16).ap()
    xn32 = DT("xn32", [D, NT], F32).ap()
    h_d = DT("h_d", [4096, NT], BF16).ap()
    g_d = DT("g_d2", [4096, NT], BF16).ap()
    hs_d = DT("hs_d", [D, NT], F32).ap()
    Srw = {}
    for f in range(5):
        Srw[f"mix{f}_d"] = DT(f"mix{f}_d", [1024, NT], BF16).ap()
    for nm, shape, dt in [("dx_d", [1024, NT], BF16), ("r_d", [1024, NT], BF16),
                          ("k_d", [1024, NT], BF16), ("g_d", [1024, NT], BF16),
                          ("gate_d", [1024, NT], BF16), ("ktm_d", [NT, 1024], BF16),
                          ("vtm_d", [NT, 1024], BF16), ("expw_d", [NT, 1024], F32),
                          ("y_d", [1024, NT], F32)]:
        Srw[nm] = DT("rw_" + nm, shape, dt).ap()
    mk = dict(kind="ExternalOutput") if dbg_stream else {}
    Smam = {"xz_d": DT("xz_d", [4096, NT], BF16, **mk).ap(),
            "xc_d": DT("xc_d", [2048, NT], BF16, **mk).ap(),
            "dt_d": DT("dt_d", [2048, NT], F32, **mk).ap(),
            "dtu_d": DT("dtu_d", [2048, NT], F32, **mk).ap(),
            "y_d": DT("mam_y_d", [2048, NT], BF16, **mk).ap(),
            "xdbl_d": DT("xdbl_d", [96, NT], BF16, **mk).ap()}
    Smoe = {"tokidx_d": DT("tokidx_d", [8 * 1024], I32).ap(),
            "ytm_d": DT("ytm_d", [NT, D], BF16).ap()}

    cst = Consts(ctx)
    sc = {nm: load_scale_col(ctx, "sc_" + nm, W[nm].shape[1])
          for nm in ()}
    def scol(nm):
        sb = nc.alloc_sbuf_tensor("sb_" + nm, list(W[nm].shape), F32).ap()
        ctx.dma("sp", sb, W[nm])
        return sb
    ffn_s1 = scol("ffn_s1"); ffn_s2 = scol("ffn_s2")
    ffn_hs1 = scol("ffn_hs1"); ffn_hs2 = scol("ffn_hs2"); ffn_s3 = scol("ffn_s3")
    mam_lnw = scol("mam_lnw2"); mam_lnb = scol("mam_lnb2")
    lat_gam = scol("lat_gam2"); lat_bet = scol("lat_bet2")
    ctx.barrier_all()

    dbgs = []
    def snap(ap, i):
        if dbg_stream:
            d = DT(f"dbg_res{i}", [D, NT], F32, kind="ExternalOutput").ap()
            ctx.barrier_all()
            ctx.dma("sp", d, ap)
            ctx.barrier_all()
    emit_ln(ctx, cst, resA, xn_fm, name="l0a_ln")
    emit_mla(ctx, cst, xn_fm, resA, resB, W, name="mla")
    snap(resB, 0)
    emit_ln(ctx, cst, resB, xn_fm, name="l0f1_ln")
    emit_ffn(ctx, cst, xn_fm, resB, resC, W["ffn_w1T"], W["ffn_w2T"], W["ffn_h1T"],
             W["ffn_h2T"], W["ffn_w3T"], ffn_s1, ffn_s2, ffn_hs1, ffn_hs2, ffn_s3,
             h_d, g_d, name="ffn1")
    snap(resC, 1)
    emit_ln(ctx, cst, resC, xn_fm, name="l0b_ln")
    emit_rwkv(ctx, cst, xn_fm, resC, resB, W, Srw, name="rw")
    snap(resB, 2)
    emit_ln(ctx, cst, resB, xn_fm, name="l0f2_ln")
    emit_ffn(ctx, cst, xn_fm, resB, resC, W["ffn_w1T"], W["ffn_w2T"], W["ffn_h1T"],
             W["ffn_h2T"], W["ffn_w3T"], ffn_s1, ffn_s2, ffn_hs1, ffn_hs2, ffn_s3,
             h_d, g_d, name="ffn2")
    snap(resC, 3)
    emit_ln(ctx, cst, resC, xn_fm, name="l1a_ln")
    emit_ln(ctx, cst, xn_fm, xn2_fm, gamma=mam_lnw, beta=mam_lnb, src_dt=BF16,
            name="l1a_ln2")
    emit_mamba(ctx, cst, xn2_fm, xn_fm, resC, resB, W, Smam, name="mam")
    snap(resB, 4)
    emit_ln(ctx, cst, resB, xn_fm, dst_tm=xn_tm, dst_f32=xn32, name="l1m1_ln")
    emit_moe(ctx, cst, xn_fm, xn_tm, xn32, resB, resC, W, Smoe, initidx, name="moe1")
    snap(resC, 5)
    emit_ln(ctx, cst, resC, xn_fm, name="l1b_ln")
    emit_lattn(ctx, cst, xn_fm, resC, resB, W, lat_gam, lat_bet, hs_d, name="lat")
    snap(resB, 6)
    emit_ln(ctx, cst, resB, xn_fm, dst_tm=xn_tm, dst_f32=xn32, name="l1m2_ln")
    emit_moe(ctx, cst, xn_fm, xn_tm, xn32, resB, out, W, Smoe, initidx, name="moe2")
    ctx.barrier_all()
    ctx.wait("sp", "dma_s", ctx.c("dma_s"))
    finalize(ctx)
    return nc


def host_inputs(inputs):
    g = inputs
    inp = dict(host_constants())
    inp["mla_WDQ"] = g["L0_attn_a_W_DQ"]
    inp["mla_WDKV"] = g["L0_attn_a_W_DKV"]
    inp["mla_WKR"] = g["L0_attn_a_W_KR"]
    inp["mla_WUQ"] = g["L0_attn_a_W_UQ"]
    inp["mla_WUK"] = g["L0_attn_a_W_UK"]
    inp["mla_WUV"] = g["L0_attn_a_W_UV"]
    inp["mla_WQR"] = g["L0_attn_a_W_QR"]
    inp["mla_WO"] = g["L0_attn_a_W_O"]
    inp["mla_bUV"] = g["L0_attn_a_b_UV"]
    inp["mla_bsmall"] = np.ascontiguousarray(np.stack(
        [g["L0_attn_a_b_DQ"], g["L0_attn_a_b_DKV"], g["L0_attn_a_b_KR"]], 1
    ).astype(np.float32))
    inp["mla_bUQ2"] = prep_scale_col(g["L0_attn_a_b_UQ"])
    inp["mla_bUK2"] = prep_scale_col(g["L0_attn_a_b_UK"])
    inp["mla_bQR2"] = np.ascontiguousarray(g["L0_attn_a_b_QR"].reshape(8, 32).T.astype(np.float32))
    inp["mla_bO2"] = prep_scale_col(g["L0_attn_a_b_O"])
    inp["ffn_w1T"] = np.ascontiguousarray(g["L0_ffn_w1_w"].T.astype(np.float32))
    inp["ffn_w2T"] = np.ascontiguousarray(g["L0_ffn_w2_w"].T.astype(np.float32))
    inp["ffn_h1T"] = np.ascontiguousarray(g["L0_ffn_h1_w"].T.astype(np.float32))
    inp["ffn_h2T"] = np.ascontiguousarray(g["L0_ffn_h2_w"].T.astype(np.float32))
    inp["ffn_w3T"] = np.ascontiguousarray(g["L0_ffn_w3_w"].T.astype(np.float32))
    inp["ffn_s1"] = prep_scale_col(g["L0_ffn_w1_s"])
    inp["ffn_s2"] = prep_scale_col(g["L0_ffn_w2_s"])
    inp["ffn_hs1"] = prep_scale_col(g["L0_ffn_h1_s"])
    inp["ffn_hs2"] = prep_scale_col(g["L0_ffn_h2_s"])
    inp["ffn_s3"] = prep_scale_col(g["L0_ffn_w3_s"])
    for nm in ("Wr", "Wk", "Wv", "Wg", "Wo"):
        inp["rw_" + nm] = g["L0_attn_b_" + nm]
    inp["rw_Wgate"] = g["L0_attn_b_gate_w"]
    inp["rw_w1cat"] = g["L0_attn_b_maa_w1"]
    inp["rw_w2cat"] = np.ascontiguousarray(g["L0_attn_b_maa_w2"].reshape(160, 1024))
    inp["rw_dw1"] = g["L0_attn_b_decay_w1"]
    inp["rw_w2aug"] = np.ascontiguousarray(np.concatenate(
        [g["L0_attn_b_decay_w2"], g["L0_attn_b_time_decay"][None, :]], 0
    ).astype(np.float32))
    maa = np.zeros((128, 48), np.float32)
    for dc in range(8):
        for i, nmv in enumerate(("maa_x", "maa_w", "maa_k", "maa_v", "maa_r", "maa_g")):
            maa[:, 6 * dc + i] = g["L0_attn_b_" + nmv][dc * 128:(dc + 1) * 128]
    inp["rw_maa"] = maa
    cwv = np.zeros((128, 24), np.float32)
    for dc in range(8):
        for j in range(3):
            cwv[:, 3 * dc + j] = g["L0_attn_b_conv_w"][dc * 128:(dc + 1) * 128, 0, j]
    inp["rw_convw"] = cwv
    inp["rw_gateb64"] = np.ascontiguousarray(g["L0_attn_b_gate_b"].reshape(16, 64).T.astype(np.float32))
    inp["rw_lnw64"] = np.ascontiguousarray(g["L0_attn_b_ln_w"].reshape(16, 64).T.astype(np.float32))
    inp["rw_lnb64"] = np.ascontiguousarray(g["L0_attn_b_ln_b"].reshape(16, 64).T.astype(np.float32))
    inp["mam_Win"] = g["L1_attn_a_W_in"]
    inp["mam_Wxp"] = g["L1_attn_a_W_xproj"]
    inp["mam_Wdt"] = g["L1_attn_a_W_dt"]
    inp["mam_Wout"] = g["L1_attn_a_W_out"]
    inp["mam_convw"] = np.ascontiguousarray(
        g["L1_attn_a_conv_w"][:, 0, :].reshape(16, 128, 4).transpose(1, 0, 2).reshape(128, 64).astype(np.float32))
    inp["mam_convb"] = prep_scale_col(g["L1_attn_a_conv_b"])
    inp["mam_bdt"] = prep_scale_col(g["L1_attn_a_b_dt"])
    inp["mam_Alog"] = np.ascontiguousarray(
        g["L1_attn_a_A_log"].reshape(16, 128, 16).transpose(1, 0, 2).reshape(128, 256).astype(np.float32))
    inp["mam_D"] = prep_scale_col(g["L1_attn_a_D_param"])
    inp["mam_lnw2"] = prep_scale_col(g["L1_attn_a_ln_w"])
    inp["mam_lnb2"] = prep_scale_col(g["L1_attn_a_ln_b"])
    inp["moe_router"] = np.ascontiguousarray(g["L1_ffn_router_w"].T.astype(np.float32))
    inp["moe_gateT"] = np.ascontiguousarray(
        g["L1_ffn_gate_w"].transpose(0, 2, 1).reshape(8 * 1024, 2816).astype(np.float32))
    inp["moe_upT"] = np.ascontiguousarray(
        g["L1_ffn_up_w"].transpose(0, 2, 1).reshape(8 * 1024, 2816).astype(np.float32))
    inp["moe_downT"] = np.ascontiguousarray(
        g["L1_ffn_down_w"].transpose(0, 2, 1).reshape(8 * 2816, 1024).astype(np.float32))
    inp["lat_WqT"] = np.ascontiguousarray(g["L1_attn_b_Wq"].T.astype(np.float32))
    inp["lat_WkT"] = np.ascontiguousarray(g["L1_attn_b_Wk"].T.astype(np.float32))
    inp["lat_WvT"] = np.ascontiguousarray(g["L1_attn_b_Wv"].T.astype(np.float32))
    inp["lat_WdT"] = np.ascontiguousarray(g["L1_attn_b_Wd"].T.astype(np.float32))
    inp["lat_bq2"] = prep_scale_col(g["L1_attn_b_bq"])
    inp["lat_bd2"] = prep_scale_col(g["L1_attn_b_bd"])
    inp["lat_bkv"] = np.ascontiguousarray(np.concatenate(
        [g["L1_attn_b_bk"], g["L1_attn_b_bv"]]).astype(np.float32))
    inp["lat_gam2"] = prep_scale_col(g["L1_attn_b_gamma"])
    inp["lat_bet2"] = prep_scale_col(g["L1_attn_b_beta"])
    return inp


def prep_scale_col(vec):
    n = vec.shape[0] // 128
    return np.ascontiguousarray(np.asarray(vec).reshape(n, 128).T.astype(np.float32))


def kernel(**inputs):
    shared = host_inputs(inputs)
    x = np.asarray(inputs["L0_attn_a_h_t"], dtype=np.float32)
    nc = build_full()
    in_maps = []
    for c in range(NCORE):
        m = dict(shared)
        xc = x[2 * c:2 * c + 2].reshape(NT, D)
        m["x_fm"] = np.ascontiguousarray(xc.T)
        in_maps.append(m)
    res = run_bass_kernel_spmd(nc, in_maps, core_ids=list(range(NCORE)))
    out = np.empty((16, 2048, 1024), dtype=np.float32)
    for c in range(NCORE):
        out[2 * c:2 * c + 2] = res.results[c]["out_fm"].T.reshape(2, T, D)
    return out
